# version 10 (final)
# baseline (speedup 1.0000x reference)
import math
import numpy as np
import concourse.bass as bass
import concourse.mybir as mybir
from concourse.bass_utils import run_bass_kernel_spmd

F32 = mybir.dt.float32
BF16 = mybir.dt.bfloat16
I32 = mybir.dt.int32
AF = mybir.ActivationFunctionType
ALU = mybir.AluOpType
AX = mybir.AxisListType

D = 2048
HID = 5632
NH = 32
NP = 16
TB = 4096
TO = 1024
INVS = 1.0 / math.sqrt(32.0)
EPS = 1e-5
NHC = 44
GRP = 8
ENGS = ("sync", "gpsimd", "vector", "scalar", "tensor")


class Sem:
    def __init__(self, h):
        self.h = h
        self.n = 0


def build(stop_after=99, debug=False, sub=99):
    nc = bass.Bass("TRN2", target_bir_lowering=False)

    xb = nc.dram_tensor("xb", [TB, D], F32, kind="ExternalInput")
    wq = nc.dram_tensor("wq", [16, D, 128], F32, kind="ExternalInput")
    wk = nc.dram_tensor("wk", [16, D, 128], F32, kind="ExternalInput")
    wv = nc.dram_tensor("wv", [16, D, 128], F32, kind="ExternalInput")
    wo = nc.dram_tensor("wo", [4, D, 512], F32, kind="ExternalInput")
    bq = nc.dram_tensor("bq", [D], F32, kind="ExternalInput")
    bk = nc.dram_tensor("bk", [D], F32, kind="ExternalInput")
    bv = nc.dram_tensor("bv", [D], F32, kind="ExternalInput")
    bo = nc.dram_tensor("bo", [D], F32, kind="ExternalInput")
    wg = nc.dram_tensor("wg", [NHC, D, 128], I32, kind="ExternalInput")
    wu = nc.dram_tensor("wu", [NHC, D, 128], I32, kind="ExternalInput")
    wd = nc.dram_tensor("wd", [NHC, 128, D], I32, kind="ExternalInput")
    sg = nc.dram_tensor("sg", [HID], F32, kind="ExternalInput")
    su = nc.dram_tensor("su", [HID], F32, kind="ExternalInput")
    sd = nc.dram_tensor("sd", [D], F32, kind="ExternalInput")
    out = nc.dram_tensor("out", [TO, D], F32, kind="ExternalOutput")

    kslab = nc.dram_tensor("kslab", [16, 128, TB], BF16)
    vslab = nc.dram_tensor("vslab", [NH, TB, 64], BF16)
    bbrow = nc.dram_tensor("bbrow", [NH], F32)
    dslab = nc.dram_tensor("dslab", [NH, TO], F32)
    dbg = {}
    if debug:
        dbg["qT"] = nc.dram_tensor("dbg_qT", [128, 16 * TO], BF16, kind="ExternalOutput")
        dbg["k"] = nc.dram_tensor("dbg_k", [16, 128, TB], BF16, kind="ExternalOutput")
        dbg["v"] = nc.dram_tensor("dbg_v", [NH, TB, 64], BF16, kind="ExternalOutput")
        dbg["b"] = nc.dram_tensor("dbg_b", [NH], F32, kind="ExternalOutput")
        dbg["f"] = nc.dram_tensor("dbg_f", [128, 16 * TO], BF16, kind="ExternalOutput")
        dbg["r1"] = nc.dram_tensor("dbg_r1", [TO, D], F32, kind="ExternalOutput")

    PB = 17920
    off = [PB]

    def alloc(name, shape, dtype, at=None):
        a = off[0] if at is None else at
        t = nc.alloc_sbuf_tensor_at(name, shape, dtype, offset=a)
        if at is None:
            per = mybir.dt.size(dtype)
            for s in shape[1:]:
                per *= s
            off[0] = a + ((per + 31) // 32) * 32
        return t

    identb = alloc("identb", [128, 128], BF16)
    identf = alloc("identf", [128, 128], F32)
    biasB = alloc("biasB", [128, NH], F32)
    bqs = alloc("bqs", [128, 16], F32)
    bks = alloc("bks", [128, 16], F32)
    bvs = alloc("bvs", [128, 16], F32)
    kmax = alloc("kmax", [128, NH], F32)
    qmax = alloc("qmax", [128, NH], F32)
    scr = alloc("scr", [128, 64], F32)
    nrow = alloc("nrow", [1, 64], F32)
    stat = alloc("stat", [128, 24], F32)
    mv = alloc("mv", [128, 2], F32)
    rstd = alloc("rstd", [128, 1], F32)
    negmr = alloc("negmr", [128, 1], F32)
    lnv = alloc("lnv", [128, 1], F32)
    sgc = alloc("sgc", [128, NHC], F32)
    suc = alloc("suc", [128, NHC], F32)
    epst = alloc("epst", [128, 1], F32)
    n128 = alloc("n128", [128, 1], F32)
    ldst = alloc("ldst", [128, 128], F32)
    lds2 = alloc("lds2", [128, 128], F32)
    P_END = off[0]
    assert P_END - PB <= 4096, P_END - PB
    R1 = ((P_END + 63) // 64) * 64
    R2 = R1 + 32768
    R3 = R2 + 32768
    SB_LIMIT = 229376
    R3SZ = SB_LIMIT - R3
    assert R3SZ >= 139264, R3SZ

    qT = alloc("qT", [128, 16 * TO], BF16, at=R1)
    featsT = alloc("featsT", [128, 16 * TO], BF16, at=R2)
    lnxT = alloc("lnxT", [128, 16 * 2048], BF16, at=R3)
    lnst = alloc("lnst", [128, 4 * D], BF16, at=R3 + 65536)
    xst = [alloc("xst0", [128, D], F32, at=R3 + 81920),
           alloc("xst1", [128, D], F32, at=R3 + 90112)]
    wst = [alloc("wst0", [128, D], F32, at=R3 + 98304),
           alloc("wst1", [128, D], F32, at=R3 + 106496)]
    wbf = [alloc("wbf0", [128, D], BF16, at=R3 + 114688),
           alloc("wbf1", [128, D], BF16, at=R3 + 118784)]
    kev = alloc("kev", [128, 2048], BF16, at=R3 + 122880)
    sqst = alloc("sqst", [128, 2048], BF16, at=R3 + 126976)
    vts = alloc("vts", [128, 2048], BF16, at=R3 + 131072)
    res = alloc("res", [128, 8 * D], F32, at=R3)
    FL = R3 + 65536
    kp = [alloc("kp0", [128, 2 * TB], BF16, at=FL),
          alloc("kp1", [128, 2 * TB], BF16, at=FL + 16384)]
    va = [alloc("va0", [128, 32 * 2 * 128], BF16, at=FL + 32768),
          alloc("va1", [128, 32 * 2 * 128], BF16, at=FL + 49152)]
    pt = [alloc("pt0", [128, TO], BF16, at=FL + 65536),
          alloc("pt1", [128, TO], BF16, at=FL + 67584)]
    nbc = [alloc("nbc0", [128, TO], F32, at=FL + 69632)]
    dsb = nbc[0][0:1, :]
    rsb = dsb
    owst = alloc("owst", [128, 16 * 512], F32, at=FL)
    owbf = [alloc("owbf0", [128, 16 * 512], BF16, at=FL + 32768),
            alloc("owbf1", [128, 16 * 512], BF16, at=FL + 49152)]
    bob = alloc("bob", [128, D], F32, at=FL + 65536)
    ln2xT = alloc("ln2xT", [128, 16 * TO], BF16, at=R1)
    gub = alloc("gub", [128, GRP * TO], BF16, at=R2)
    ln2st = alloc("ln2st", [128, 4 * D], BF16, at=R2 + 16384)
    wdbA = alloc("wdbA", [128, 4 * D], BF16, at=R2 + 16384)
    wgi = [alloc("wgi0", [128, D], I32, at=FL),
           alloc("wgi1", [128, D], I32, at=FL + 8192)]
    wui = [alloc("wui0", [128, D], I32, at=FL + 16384),
           alloc("wui1", [128, D], I32, at=FL + 24576)]
    wdi = alloc("wdi", [128, D], I32, at=FL + 32768)
    wgb = alloc("wgb", [128, D], BF16, at=FL + 40960)
    wub = alloc("wub", [128, D], BF16, at=FL + 45056)
    wdbB = alloc("wdbB", [128, 4 * D], BF16, at=FL + 49152)
    sdb = alloc("sdb", [128, D], F32, at=FL + 65536)
    gsil = alloc("gsil", [128, TO], BF16, at=FL + 73728)
    assert FL + 73728 <= SB_LIMIT - 32, (FL, SB_LIMIT)

    PSA = nc.alloc_psum_tensor("PSA", [128, 4096], F32)

    def bkf(b, n=1):
        return PSA[:, b * 512:(b + n) * 512]

    def bkb(b, n=1):
        return PSA[:, b * 512:(b + n) * 512].bitcast(BF16)

    from contextlib import ExitStack
    ctx = ExitStack()
    S = {n: Sem(ctx.enter_context(nc.semaphore(n)))
         for n in ["dW0", "dW1", "dX0", "dX1", "dK", "dF0", "dF1", "dB",
                   "dR", "dRs", "dO", "tS", "sS", "vS", "gS"]}
    dK, dB, dR, dRs, dO, tS, sS, vS, gS = (S[k] for k in
        ["dK", "dB", "dR", "dRs", "dO", "tS", "sS", "vS", "gS"])
    dWp = [S["dW0"], S["dW1"]]
    dXp = [S["dX0"], S["dX1"]]
    dFp = [S["dF0"], S["dF1"]]

    steps = []

    def st(eng, fn, waits=(), inc=None, k=1):
        w = [(s.h, v) for (s, v) in waits if v > 0]
        incs = []
        r = None
        if inc is not None:
            incs = [(inc, k)]
            inc.n += k
            r = inc.n
        steps.append((eng, fn, w, incs))
        return r

    vav = [t.ap().rearrange("p (c h f) -> p c h f", h=2, f=128) for t in va]

    for t_, val in ((identb[:, :], 0.0), (identf[:, :], 0.0),
                    (kmax[:, :], 0.0), (qmax[:, :], 0.0),
                    (epst[:, :], EPS), (n128[:, :], -128.0),
                    (ldst[:, :], 0.0), (lds2[:, :], 0.0)):
        st("gpsimd", lambda g, tt=t_, val=val: g.memset(tt, val), inc=gS)
    mset = gS.n
    st("gpsimd", lambda g: g.affine_select(
        out=identb[:, :], in_=identb[:, :], compare_op=ALU.not_equal,
        fill=1.0, base=0, pattern=[[-1, 128]], channel_multiplier=1),
        waits=[(gS, mset)], inc=gS)
    st("gpsimd", lambda g: g.affine_select(
        out=identf[:, :], in_=identf[:, :], compare_op=ALU.not_equal,
        fill=1.0, base=0, pattern=[[-1, 128]], channel_multiplier=1),
        waits=[(gS, mset)], inc=gS)
    ident_done = gS.n
    if sub < 2:
        setup_done = sS.n
    if sub >= 2:
        st("gpsimd", lambda g: g.dma_start(
            out=ldst[0:16, 0:128], in_=bq.ap().rearrange("(a c) -> a c", c=128)),
            waits=[(gS, ident_done)], inc=dR, k=16)
    if sub >= 2:
        st("gpsimd", lambda g: g.dma_start(
            out=ldst[32:48, 0:128], in_=bk.ap().rearrange("(a c) -> a c", c=128)), inc=dR, k=16)
        st("gpsimd", lambda g: g.dma_start(
            out=ldst[64:80, 0:128], in_=bv.ap().rearrange("(a c) -> a c", c=128)), inc=dR, k=16)
        st("gpsimd", lambda g: g.dma_start(
            out=lds2[0:44, 0:128], in_=sg.ap().rearrange("(a c) -> a c", c=128)), inc=dR, k=16)
        st("gpsimd", lambda g: g.dma_start(
            out=lds2[64:108, 0:128], in_=su.ap().rearrange("(a c) -> a c", c=128)), inc=dR, k=16)
        rows_ld = dR.n
        setup_done = sS.n
    if sub >= 3:
        st("tensor", lambda t: t.transpose(bkf(0)[:, 0:128], ldst[:, :], identf[:, :]),
           waits=[(gS, ident_done), (dR, rows_ld)], inc=tS)
        st("tensor", lambda t: t.transpose(bkf(2)[:, 0:128], lds2[:, :], identf[:, :]), inc=tS)
        tr_done = tS.n
    if sub >= 4:
        st("scalar", lambda s: s.activation(out=bqs[:, :], in_=bkf(0)[:, 0:16],
                                            func=AF.Copy, bias=0.0, scale=INVS),
           waits=[(tS, tr_done)], inc=sS)
        st("scalar", lambda s: s.copy(out=bks[:, :], in_=bkf(0)[:, 32:48]), inc=sS)
        st("scalar", lambda s: s.copy(out=bvs[:, :], in_=bkf(0)[:, 64:80]), inc=sS)
        st("scalar", lambda s: s.copy(out=sgc[:, :], in_=bkf(2)[:, 0:NHC]), inc=sS)
        st("scalar", lambda s: s.copy(out=suc[:, :], in_=bkf(2)[:, 64:64 + NHC]), inc=sS)
        setup_done = sS.n

    nrm_free = [0]
    nrm_tr = [0]

    def _norms(sqtile, ntiles, maxdst, fo, sqg):
        nb = ntiles // 4
        for b4 in range(nb):
            bank = 4 + b4
            boff = 0
            for j in range(4):
                st("tensor", lambda t, b4=b4, j=j, bank=bank, boff=boff, sqtile=sqtile: t.transpose(
                    bkb(bank)[:, boff + j * 128:boff + (j + 1) * 128],
                    sqtile[:, (b4 * 4 + j) * 128:(b4 * 4 + j + 1) * 128],
                    identb[:, :]),
                    waits=[(vS, sqg), (vS, nrm_free[0])] if (j == 0 and b4 == 0) else [],
                    inc=tS if j == 3 else None)
            trc = tS.n
            st("vector", lambda v, bank=bank, boff=boff: v.reduce_sum(
                out=scr.ap().rearrange("p (a b) -> p a b", b=2)[:, 0:4, :],
                in_=bkb(bank)[:, boff:boff + 512].rearrange("p (i h f) -> p i h f", h=2, f=64),
                axis=AX.X), waits=[(tS, trc), (vS, vS.n)], inc=vS)
            st("vector", lambda v: v.reduce_max(
                out=scr[:, 8:10],
                in_=scr.ap().rearrange("p (a b) -> p b a", b=2)[:, :, 0:4],
                axis=AX.X), waits=[(vS, vS.n)], inc=vS)
            st("vector", lambda v, fo=fo, maxdst=maxdst: v.tensor_max(
                maxdst[:, 2 * fo:2 * fo + 2], maxdst[:, 2 * fo:2 * fo + 2],
                scr[:, 8:10]), waits=[(vS, vS.n)], inc=vS)
        nrm_free[0] = vS.n

    Wmap = {"q": (wq, bqs, INVS), "k": (wk, bks, 1.0), "v": (wv, bvs, 1.0)}
    lnx_guard = {}
    kv_end = {}
    kv_done = (0, 0, 0, 0)
    bias_rdy = 0
    if stop_after >= 1:
        for h in (range(2) if sub >= 11 else []):
            prev_cp = vS.n
            for tt in range(16):
                b4, j4 = tt // 4, tt % 4
                g_x = lnx_guard.get((h, tt - 2), lnx_guard.get((h - 1, 14 + tt), 0))
                ldx = st("sync", lambda e, h=h, tt=tt: e.dma_start(
                    out=xst[tt % 2][:, :],
                    in_=xb[h * 2048 + tt * 128: h * 2048 + (tt + 1) * 128, :]),
                    waits=[(sS, g_x)], inc=dXp[tt % 2], k=16)
                for i in range(4):
                    st("vector", lambda v, tt=tt, i=i: v.bn_stats(
                        out=stat[:, 6 * i:6 * i + 6], in_=xst[tt % 2][:, 512 * i:512 * (i + 1)]),
                        waits=[(dXp[tt % 2], ldx), (vS, vS.n)] if i == 0 else [], inc=vS)
                st("vector", lambda v: v.bn_aggr(out=mv[:, :], in_=stat[:, :]),
                   waits=[(vS, vS.n)], inc=vS)
                ag = vS.n
                st("scalar", lambda s: s.activation(out=lnv[:, :], in_=mv[:, 1:2],
                                                    func=AF.Ln, bias=epst[:, :], scale=1.0),
                   waits=[(vS, ag), (sS, sS.n)], inc=sS)
                st("scalar", lambda s: s.activation(out=rstd[:, :], in_=lnv[:, :],
                                                    func=AF.Exp, bias=0.0, scale=-0.5),
                   waits=[(sS, sS.n)], inc=sS)
                rs = sS.n
                st("vector", lambda v: v.scalar_tensor_tensor(
                    out=negmr[:, :], in0=mv[:, 0:1], scalar=-1.0, in1=rstd[:, :],
                    op0=ALU.mult, op1=ALU.mult), waits=[(sS, rs), (vS, vS.n)], inc=vS)
                nm = vS.n
                lnslot = lnst[:, j4 * D:(j4 + 1) * D]
                lg = st("scalar", lambda s, lnslot=lnslot, tt=tt: s.activation(
                    out=lnslot, in_=xst[tt % 2][:, :], func=AF.Identity,
                    bias=negmr[:, :], scale=rstd[:, :]),
                    waits=[(vS, nm), (vS, prev_cp), (sS, sS.n)], inc=sS)
                lnx_guard[(h, tt)] = lg
                if j4 == 3:
                    lnrdy = lg
                    cp_g = {}
                    for fi in range(16):
                        bank = 4 + (fi % 2)
                        for j in range(4):
                            st("tensor", lambda t, fi=fi, j=j, bank=bank: t.transpose(
                                bkb(bank)[:, j * 128:(j + 1) * 128],
                                lnst[:, j * D + fi * 128: j * D + (fi + 1) * 128],
                                identb[:, :]),
                                waits=([(sS, lnrdy), (vS, cp_g.get(fi - 2, prev_cp))]
                                       if j == 0 else []),
                                inc=tS if j == 3 else None)
                        trc = tS.n
                        cp_g[fi] = st("vector", lambda v, fi=fi, bank=bank, h=h, b4=b4: v.tensor_copy(
                            out=lnxT[:, fi * 2048 + b4 * 512: fi * 2048 + (b4 + 1) * 512],
                            in_=bkb(bank)[:, 0:512]),
                            waits=[(tS, trc), (vS, vS.n)], inc=vS)
                    prev_cp = vS.n
            ln_done = vS.n

            if sub <= 11:
                ws = []
            elif sub <= 13:
                ws = ["k"]
            elif sub == 14:
                ws = ["k", "v"]
            else:
                ws = (["k", "v", "q"] if h == 0 else ["k", "v"])
            cast_g = {}
            evac_g = {}
            mm_end = {}
            for iw, w in enumerate(ws):
                wt, bcol, wscale = Wmap[w]
                ntq = 1 if w == "q" else 2

                def ldcast(iw, fo, wt):
                    pg = cast_g.get((iw, fo - 2), cast_g.get((iw - 1, 14 + fo), 0))
                    ldw = st("sync", lambda e, wt=wt, fo=fo: e.dma_start(
                        out=wst[fo % 2].ap().rearrange("p (c x) -> p c x", x=128),
                        in_=wt[fo].rearrange("(c p) x -> p c x", p=128)),
                        waits=[(sS, pg)], inc=dWp[fo % 2], k=16)
                    mmg = mm_end.get((iw, fo - 2), mm_end.get((iw - 1, 14 + fo), 0))
                    cg = st("scalar", lambda s, fo=fo: s.copy(
                        out=wbf[fo % 2][:, :], in_=wst[fo % 2][:, :]),
                        waits=[(dWp[fo % 2], ldw), (tS, mmg), (sS, sS.n)], inc=sS)
                    cast_g[(iw, fo)] = cg

                if (iw, 0) not in cast_g:
                    ldcast(iw, 0, wt)
                for fo in range(16):
                    if fo + 1 < 16:
                        ldcast(iw, fo + 1, wt)
                    elif iw + 1 < len(ws):
                        ldcast(iw + 1, 0, Wmap[ws[iw + 1]][0])
                    cg = cast_g[(iw, fo)]
                    for tq in range(ntq):
                        pb = 0 if tq == 0 else 2
                        ev_p = evac_g.get((iw, fo - 1, tq),
                                          evac_g.get((iw - 1, 15, tq), 0))
                        for fi in range(16):
                            for nh in range(2):
                                mvw = lnxT[:, fi * 2048 + tq * 1024 + nh * 512:
                                           fi * 2048 + tq * 1024 + (nh + 1) * 512]
                                st("tensor", lambda t, pb=pb, nh=nh, fo=fo, fi=fi, mvw=mvw: t.matmul(
                                    bkf(pb + nh), wbf[fo % 2][:, fi * 128:(fi + 1) * 128],
                                    mvw, start=(fi == 0), stop=(fi == 15)),
                                    waits=([(sS, cg), (vS, ln_done), (sS, ev_p)]
                                           if fi == 0 and nh == 0 else []),
                                    inc=tS if (fi == 15 and nh == 1) else None)
                        mmc = tS.n
                        mm_end[(iw, fo)] = mmc
                        if w == "q":
                            dst = qT[:, fo * 1024:(fo + 1) * 1024]
                        else:
                            dst = kev[:, tq * 1024:(tq + 1) * 1024]
                        kevg = kv_end.get((h, iw, fo - 1),
                                           kv_end.get((h, iw - 1, 15),
                                                      kv_end.get((h - 1, 2, 15), 0)))
                        ev = st("scalar", lambda s, dst=dst, pb=pb, bcol=bcol, fo=fo,
                                wscale=wscale: s.activation(
                                    out=dst, in_=bkf(pb, 2), func=AF.Identity,
                                    bias=bcol[:, fo:fo + 1], scale=wscale),
                                waits=[(tS, mmc), (dK, kevg), (sS, sS.n)], inc=sS)
                        evac_g[(iw, fo, tq)] = ev
                    evd = sS.n
                    if w == "k":
                        st("gpsimd", lambda g, fo=fo, h=h: g.dma_start(
                            out=kslab[fo][:, h * 2048:(h + 1) * 2048], in_=kev[:, :]),
                            waits=[(sS, evd)], inc=dK, k=16)
                        sqg = st("vector", lambda v: v.tensor_mul(
                            sqst[:, :], kev[:, :], kev[:, :]),
                            waits=[(sS, evd), (vS, vS.n), (tS, nrm_tr[0])], inc=vS)
                        if sub >= 13:
                            _norms(sqst, 16, kmax, fo, sqg)
                        nrm_tr[0] = tS.n
                        kv_end[(h, iw, fo)] = dK.n
                    elif w == "v":
                        for b8 in range(2):
                            bank = 4 + b8
                            for i in range(8):
                                st("tensor", lambda t, b8=b8, i=i, bank=bank: t.transpose(
                                    bkb(bank)[:, i * 128:(i + 1) * 128],
                                    kev[:, (b8 * 8 + i) * 128:(b8 * 8 + i + 1) * 128],
                                    identb[:, :]),
                                    waits=[(sS, evd), (sS, sS.n), (vS, vS.n)] if i == 0 else [],
                                    inc=tS if i == 7 else None)
                            trc = tS.n
                            st("vector", lambda v, b8=b8, bank=bank: v.tensor_copy(
                                out=vts[:, b8 * 1024:(b8 + 1) * 1024], in_=bkb(bank)),
                                waits=[(tS, trc),
                                       (dK, kv_end.get((h, iw, fo - 1),
                                            kv_end.get((h, iw - 1, 15),
                                            kv_end.get((h - 1, 2, 15), 0)))),
                                       (vS, vS.n)], inc=vS)
                        vtd = vS.n
                        for h2 in range(2):
                            for tt8 in range(2):
                                src = vts.ap().rearrange("p (c x) -> p c x", x=128)[
                                    :, tt8 * 8:(tt8 + 1) * 8, h2 * 64:(h2 + 1) * 64]
                                dsl = vslab[2 * fo + h2].rearrange("(c p) f -> p c f", p=128)[
                                    :, h * 16 + tt8 * 8: h * 16 + (tt8 + 1) * 8, :]
                                st("gpsimd", lambda g, src=src, dsl=dsl: g.dma_start(
                                    out=dsl, in_=src),
                                    waits=[(vS, vtd)] if h2 == 0 and tt8 == 0 else [],
                                    inc=dK, k=16)
                        kv_end[(h, iw, fo)] = dK.n
                    else:
                        sqg = st("vector", lambda v, fo=fo: v.tensor_mul(
                            sqst[:, 0:1024], qT[:, fo * 1024:(fo + 1) * 1024],
                            qT[:, fo * 1024:(fo + 1) * 1024]),
                            waits=[(sS, evd), (vS, vS.n), (tS, nrm_tr[0])], inc=vS)
                        if sub >= 13:
                            _norms(sqst, 8, qmax, fo, sqg)
                        nrm_tr[0] = tS.n
                        kv_end[(h, iw, fo)] = dK.n

        kvtail = [(tS, tS.n), (sS, sS.n), (vS, vS.n)]
    if stop_after >= 1 and sub >= 16:
        st("tensor", lambda t: t.transpose(PSA[0:32, 6 * 512:6 * 512 + 128],
                                           kmax[:, 0:32], identf[:, :]),
           waits=kvtail, inc=tS)
        st("tensor", lambda t: t.transpose(PSA[0:32, 6 * 512 + 128:6 * 512 + 256],
                                           qmax[:, 0:32], identf[:, :]), inc=tS)
        trn = tS.n
        st("vector", lambda v: v.reduce_max(out=scr[0:32, 0:1],
                                            in_=PSA[0:32, 6 * 512:6 * 512 + 128], axis=AX.X),
           waits=[(tS, trn), (vS, vS.n)], inc=vS)
        st("vector", lambda v: v.reduce_max(out=scr[0:32, 1:2],
                                            in_=PSA[0:32, 6 * 512 + 128:6 * 512 + 256], axis=AX.X),
           inc=vS)
        st("vector", lambda v: v.tensor_mul(scr[0:32, 2:3], scr[0:32, 0:1], scr[0:32, 1:2]),
           waits=[(vS, vS.n)], inc=vS)
        mx = vS.n
        st("scalar", lambda s: s.activation(out=scr[0:32, 3:4], in_=scr[0:32, 2:3],
                                            func=AF.Ln, bias=0.0, scale=1.0),
           waits=[(vS, mx), (sS, sS.n)], inc=sS)
        st("scalar", lambda s: s.activation(out=scr[0:32, 4:5], in_=scr[0:32, 3:4],
                                            func=AF.Exp, bias=0.0, scale=0.5),
           waits=[(sS, sS.n)], inc=sS)
        sq_ = sS.n
        st("vector", lambda v: v.tensor_scalar(
            out=scr[0:32, 0:1], in0=scr[0:32, 4:5], scalar1=1.0, scalar2=-1.0,
            op0=ALU.add, op1=ALU.mult), waits=[(sS, sq_), (vS, vS.n)], inc=vS)
        ng = vS.n
        st("tensor", lambda t: t.transpose(PSA[0:64, 7 * 512:7 * 512 + 128],
                                           scr[:, :], identf[:, :]),
           waits=[(vS, ng)], inc=tS)
        st("scalar", lambda s: s.copy(out=nrow[0:1, 0:32],
                                      in_=PSA[0:1, 7 * 512:7 * 512 + 32]),
           waits=[(tS, tS.n), (sS, sS.n)], inc=sS)
        nr = sS.n
        st("gpsimd", lambda g: g.dma_start(out=bbrow.ap().unsqueeze(0), in_=nrow[0:1, 0:32]),
           waits=[(sS, nr)], inc=dB, k=16)
        st("gpsimd", lambda g: g.dma_start(
            out=biasB[:, :], in_=bbrow.ap().unsqueeze(0).partition_broadcast(128)),
           waits=[(dB, dB.n)], inc=dB, k=16)
        bias_rdy = dB.n
    if stop_after >= 1:
        kv_done = (tS.n, sS.n, vS.n, dK.n)

    flash_done = (0, 0, 0)
    res_rdy = 0
    if stop_after >= 2:
        kvw = [(tS, kv_done[0]), (sS, kv_done[1]), (vS, kv_done[2])]
        st("gpsimd", lambda g: g.memset(kp[0][:, :], 0.0),
           waits=kvw + [(dK, kv_done[3])], inc=gS)
        st("gpsimd", lambda g: g.memset(kp[1][:, :], 0.0), inc=gS)
        st("gpsimd", lambda g: g.memset(va[0][:, :], 0.0), inc=gS)
        st("gpsimd", lambda g: g.memset(va[1][:, :], 0.0), inc=gS)
        zro = gS.n
        st("gpsimd", lambda g: g.memset(vav[0][:, :, :, 64:65], 1.0),
           waits=[(gS, zro)], inc=gS)
        st("gpsimd", lambda g: g.memset(vav[1][:, :, :, 64:65], 1.0), inc=gS)
        va_ones = gS.n
        for i in range(8):
            st("sync", lambda e, i=i: e.dma_start(
                out=res[:, i * D:(i + 1) * D], in_=xb[i * 128:(i + 1) * 128, :]),
                waits=kvw if i == 0 else [], inc=dRs, k=16)
        res_s = dRs.n
        res_rdy = (res_s, 0)

        pair_end = {}

        def prefetch(p):
            wts = [(tS, pair_end.get(p - 2, -1))]
            if p <= 1:
                wts += [(dK, kv_done[3]), (gS, va_ones)]
            st("sync", lambda e, p=p: e.dma_start(
                out=kp[p % 2][0:64, 0:TB], in_=kslab[p][0:64, :]),
               waits=wts, inc=dFp[p % 2], k=16)
            st("sync", lambda e, p=p: e.dma_start(
                out=kp[p % 2][64:128, TB:2 * TB], in_=kslab[p][64:128, :]),
               inc=dFp[p % 2], k=16)
            for h2 in range(2):
                for hf in range(4):
                    src = vslab[2 * p + h2].rearrange("(c p) f -> p c f", p=128)[
                        :, hf * 8:(hf + 1) * 8, :]
                    dstv = vav[p % 2][:, hf * 8:(hf + 1) * 8, h2, 0:64]
                    st("sync", lambda e, src=src, dstv=dstv: e.dma_start(
                        out=dstv, in_=src), inc=dFp[p % 2], k=16)

        prefetch(0)
        prefetch(1)
        fn_guard = {}
        nrm_g = {}
        rc_g = {}
        rsb_dma = {}
        prev_exp30 = 0
        prev_eb = 0
        prev_av = None
        for p in range(NP):
            need_dF = 160 * (p // 2 + 1)
            for h2 in range(2):
                d = 2 * p + h2
                acb = 4 if h2 == 0 else 6
                exp_cnt = {}
                av_cnt = {}
                qk_cnt = {}

                def emit_qk(c, wts):
                    for qh in range(2):
                        st("tensor", lambda t, p=p, h2=h2, c=c, qh=qh: t.matmul(
                            bkf((0 if c % 2 == 0 else 2) + qh),
                            kp[p % 2][:, h2 * TB + c * 128: h2 * TB + (c + 1) * 128],
                            qT[:, p * 1024 + qh * 512: p * 1024 + (qh + 1) * 512],
                            start=True, stop=True),
                            waits=wts if qh == 0 else [], inc=tS if qh == 1 else None)
                    qk_cnt[c] = tS.n

                emit_qk(0, [(dFp[p % 2], need_dF), (sS, prev_exp30),
                            (dB, bias_rdy), (gS, va_ones)])
                emit_qk(1, [(sS, prev_eb)])
                for c in range(32):
                    avw = [(tS, qk_cnt[c])]
                    if c >= 2:
                        avw.append((tS, av_cnt[c - 2]))
                    elif prev_av is not None:
                        avw.append((tS, prev_av[30 + c]))
                    e = st("scalar", lambda s, p=p, c=c, d=d: s.activation(
                        out=pt[c % 2][:, :], in_=bkf(0 if c % 2 == 0 else 2, 2),
                        func=AF.Exp, bias=biasB[:, d:d + 1], scale=1.0),
                        waits=avw, inc=sS)
                    exp_cnt[c] = e
                    if c + 2 <= 31:
                        emit_qk(c + 2, [(sS, e)])
                    avw2 = [(sS, e)]
                    if c == 0:
                        avw2.append((vS, fn_guard.get((p, h2), 0)))
                    for qh in range(2):
                        st("tensor", lambda t, p=p, h2=h2, c=c, qh=qh, acb=acb: t.matmul(
                            PSA[:, (acb + qh) * 512:(acb + qh + 1) * 512],
                            vav[p % 2][:, c, h2, 0:128],
                            pt[c % 2][:, qh * 512:(qh + 1) * 512],
                            start=(c == 0), stop=(c == 31)),
                            waits=avw2 if qh == 0 else [],
                            inc=tS if qh == 1 else None)
                    av_cnt[c] = tS.n
                cd = st("scalar", lambda s, acb=acb: s.copy(
                    out=dsb, in_=PSA[64:65, acb * 512:(acb + 2) * 512]),
                    waits=[(tS, av_cnt[31]), (sS, sS.n), (vS, nrm_g.get(d - 1, 0)),
                           (dR, rsb_dma.get(d - 1, 0))],
                    inc=sS)
                fg = st("vector", lambda v, p=p, h2=h2, acb=acb: v.tensor_copy(
                    out=featsT[h2 * 64:(h2 + 1) * 64, p * 1024:(p + 1) * 1024],
                    in0=PSA[0:64, acb * 512:(acb + 2) * 512]) if False else v.tensor_copy(
                    out=featsT[h2 * 64:(h2 + 1) * 64, p * 1024:(p + 1) * 1024],
                    in_=PSA[0:64, acb * 512:(acb + 2) * 512]),
                    waits=[(tS, av_cnt[31]), (vS, vS.n)], inc=vS)
                fn_guard[(p + 1, h2)] = fg
                rc = st("vector", lambda v: v.reciprocal(out=rsb, in_=dsb),
                        waits=[(sS, cd), (vS, vS.n), (dR, rsb_dma.get(d - 1, 0))], inc=vS)
                rc_g[d] = rc
                st("gpsimd", lambda g, d=d: g.dma_start(
                    out=dslab[d].unsqueeze(0), in_=rsb),
                    waits=[(vS, rc)], inc=dR, k=16)
                rsb_dma[d] = dR.n
                st("gpsimd", lambda g, d=d, h2=h2: g.dma_start(
                    out=nbc[0][h2 * 64:(h2 + 1) * 64, :],
                    in_=dslab[d].unsqueeze(0).partition_broadcast(64)),
                    waits=[(dR, dR.n), (vS, nrm_g.get(d - 1, 0))], inc=dR, k=16)
                bcd = dR.n
                ng_ = st("vector", lambda v, p=p, h2=h2, d=d: v.tensor_tensor(
                    out=featsT[h2 * 64:(h2 + 1) * 64, p * 1024:(p + 1) * 1024],
                    in0=featsT[h2 * 64:(h2 + 1) * 64, p * 1024:(p + 1) * 1024],
                    in1=nbc[0][h2 * 64:(h2 + 1) * 64, :], op=ALU.mult),
                    waits=[(dR, bcd), (vS, fg), (vS, vS.n)], inc=vS)
                nrm_g[d] = ng_
                prev_exp30 = exp_cnt[30]
                prev_eb = exp_cnt[31]
                prev_av = av_cnt
                if h2 == 1:
                    pair_end[p] = tS.n
                    if p + 2 < NP:
                        prefetch(p + 2)
        flash_done = (tS.n, sS.n, vS.n)

    o_done = (0, 0, 0)
    if stop_after >= 3:
        st("gpsimd", lambda g: g.dma_start(
            out=bob[:, :], in_=bo.ap().unsqueeze(0).partition_broadcast(128)),
            waits=[(sS, flash_done[1]), (vS, flash_done[2]), (tS, flash_done[0])],
            inc=dR, k=16)
        bob_ld = dR.n
        st("vector", lambda v: v.tensor_add(res[:, 0:D], res[:, 0:D], bob[:, :]),
           waits=[(dRs, res_rdy[0]), (dR, bob_ld), (vS, flash_done[2])], inc=vS)
        for i in range(1, 8):
            st("vector", lambda v, i=i: v.tensor_add(
                res[:, i * D:(i + 1) * D], res[:, i * D:(i + 1) * D], bob[:, :]), inc=vS)
        bob_done = vS.n
        ocast_g = {}
        omm_g = {}
        oev_g = {}
        for fob in range(4):
            wts = [(sS, ocast_g.get(fob - 1, 0))]
            if fob == 0:
                wts += [(tS, flash_done[0]), (sS, flash_done[1]), (vS, flash_done[2])]
            ldw = st("sync", lambda e, fob=fob: e.dma_start(
                out=owst.ap().rearrange("p (c x) -> p c x", x=512),
                in_=wo[fob].rearrange("(c p) x -> p c x", p=128)),
                waits=wts, inc=dWp[fob % 2], k=16)
            cg = st("scalar", lambda s, fob=fob: s.copy(
                out=owbf[fob % 2][:, :], in_=owst[:, :]),
                waits=[(dWp[fob % 2], ldw), (tS, omm_g.get((fob - 2, 7), flash_done[0])),
                       (sS, flash_done[1]), (sS, sS.n)], inc=sS)
            ocast_g[fob] = cg
            for qt in range(8):
                pb = qt % 2
                guard = oev_g.get((fob, qt - 2), oev_g.get((fob - 1, qt + 6), bob_done))
                for fi in range(16):
                    st("tensor", lambda t, fob=fob, qt=qt, fi=fi, pb=pb: t.matmul(
                        bkf(pb), featsT[:, fi * 1024 + qt * 128: fi * 1024 + (qt + 1) * 128],
                        owbf[fob % 2][:, fi * 512:(fi + 1) * 512],
                        start=(fi == 0), stop=(fi == 15)),
                        waits=([(sS, cg), (vS, guard), (sS, flash_done[1])]
                               if fi == 0 else []),
                        inc=tS if fi == 15 else None)
                omm_g[(fob, qt)] = tS.n
                ev = st("vector", lambda v, fob=fob, qt=qt, pb=pb: v.scalar_tensor_tensor(
                    out=res[:, qt * D + fob * 512: qt * D + (fob + 1) * 512],
                    in0=bkf(pb), scalar=0.0,
                    in1=res[:, qt * D + fob * 512: qt * D + (fob + 1) * 512],
                    op0=ALU.add, op1=ALU.add),
                    waits=[(tS, tS.n), (vS, vS.n)], inc=vS)
                oev_g[(fob, qt)] = ev
        o_done = (tS.n, sS.n, vS.n)

    ffn_done = (0, 0, 0)
    if stop_after >= 4:
        st("gpsimd", lambda g: g.dma_start(
            out=sdb[:, :], in_=sd.ap().unsqueeze(0).partition_broadcast(128)),
            waits=[(vS, o_done[2])], inc=dR, k=16)
        sdb_rdy = dR.n
        prev_cp = o_done[2]
        for tt in range(8):
            b4, j4 = tt // 4, tt % 4
            rv = res[:, tt * D:(tt + 1) * D]
            for i in range(4):
                st("vector", lambda v, rv=rv, i=i: v.bn_stats(
                    out=stat[:, 6 * i:6 * i + 6], in_=rv[:, 512 * i:512 * (i + 1)]),
                    waits=[(vS, vS.n)] if i == 0 else [], inc=vS)
            st("vector", lambda v: v.bn_aggr(out=mv[:, :], in_=stat[:, :]),
               waits=[(vS, vS.n)], inc=vS)
            ag = vS.n
            st("scalar", lambda s: s.activation(out=lnv[:, :], in_=mv[:, 1:2],
                                                func=AF.Ln, bias=epst[:, :], scale=1.0),
               waits=[(vS, ag), (sS, sS.n), (sS, o_done[1])], inc=sS)
            st("scalar", lambda s: s.activation(out=rstd[:, :], in_=lnv[:, :],
                                                func=AF.Exp, bias=0.0, scale=-0.5),
               waits=[(sS, sS.n)], inc=sS)
            rs = sS.n
            st("vector", lambda v: v.scalar_tensor_tensor(
                out=negmr[:, :], in0=mv[:, 0:1], scalar=-1.0, in1=rstd[:, :],
                op0=ALU.mult, op1=ALU.mult), waits=[(sS, rs), (vS, vS.n)], inc=vS)
            nm = vS.n
            lnslot = ln2st[:, j4 * D:(j4 + 1) * D]
            lg = st("scalar", lambda s, lnslot=lnslot, rv=rv: s.activation(
                out=lnslot, in_=rv, func=AF.Identity, bias=negmr[:, :], scale=rstd[:, :]),
                waits=[(vS, nm), (vS, prev_cp), (sS, sS.n)], inc=sS)
            if j4 == 3:
                cp_g = {}
                for fi in range(16):
                    bank = 4 + (fi % 2)
                    for j in range(4):
                        st("tensor", lambda t, fi=fi, j=j, bank=bank: t.transpose(
                            bkb(bank)[:, j * 128:(j + 1) * 128],
                            ln2st[:, j * D + fi * 128: j * D + (fi + 1) * 128],
                            identb[:, :]),
                            waits=([(sS, lg), (vS, cp_g.get(fi - 2, prev_cp)),
                                    (tS, o_done[0])] if j == 0 else []),
                            inc=tS if j == 3 else None)
                    trc = tS.n
                    cp_g[fi] = st("vector", lambda v, fi=fi, bank=bank, b4=b4: v.tensor_copy(
                        out=ln2xT[:, fi * 1024 + b4 * 512: fi * 1024 + (b4 + 1) * 512],
                        in_=bkb(bank)[:, 0:512]),
                        waits=[(tS, trc), (vS, vS.n)], inc=vS)
                prev_cp = vS.n
        ln2_done = vS.n

        deq_g = {}
        mm_g = {}
        sil_g = {}
        gum_g = {}
        dwn_g = {}
        dev_g = {}
        groups = [list(range(g0, min(g0 + GRP, NHC))) for g0 in range(0, NHC, GRP)]
        for gi, grp in enumerate(groups):
            for jj, hc in enumerate(grp):
                ow = ([(sS, o_done[1]), (tS, o_done[0]), (vS, o_done[2])]
                      if hc < 2 else [])
                lg1 = st("sync", lambda e, hc=hc: e.dma_start(
                    out=wgi[hc % 2].ap().rearrange("p (c x) -> p c x", x=128),
                    in_=wg[hc].rearrange("(c p) x -> p c x", p=128)),
                    waits=[(sS, deq_g.get(("g", hc - 2), 0))] + ow,
                    inc=dWp[hc % 2], k=16)
                lg2 = st("sync", lambda e, hc=hc: e.dma_start(
                    out=wui[hc % 2].ap().rearrange("p (c x) -> p c x", x=128),
                    in_=wu[hc].rearrange("(c p) x -> p c x", p=128)),
                    waits=[(sS, deq_g.get(("u", hc - 2), 0))], inc=dWp[hc % 2], k=16)
                lg3 = st("sync", lambda e, hc=hc: e.dma_start(
                    out=wdi[:, :], in_=wd[hc][:, :]),
                    waits=[(vS, deq_g.get(("d", hc - 1), 0))], inc=dWp[hc % 2], k=16)
                dg = st("scalar", lambda s, hc=hc: s.activation(
                    out=wgb[:, :], in_=wgi[hc % 2][:, :], func=AF.Identity,
                    bias=n128[:, :], scale=1.0),
                    waits=[(dWp[hc % 2], lg3), (tS, mm_g.get(("g", hc - 1), 0)),
                           (vS, ln2_done), (sS, sS.n)], inc=sS)
                deq_g[("g", hc)] = dg
                du = st("scalar", lambda s, hc=hc: s.activation(
                    out=wub[:, :], in_=wui[hc % 2][:, :], func=AF.Identity,
                    bias=n128[:, :], scale=1.0),
                    waits=[(dWp[hc % 2], lg3), (tS, mm_g.get(("u", hc - 1), 0)), (sS, sS.n)],
                    inc=sS)
                deq_g[("u", hc)] = du
                wdbuf = wdbA if jj < 4 else wdbB
                wslot = (jj % 4) * D
                dd = st("vector", lambda v, wdbuf=wdbuf, wslot=wslot: v.scalar_tensor_tensor(
                    out=wdbuf[:, wslot:wslot + D], in0=wdi[:, :], scalar=-128.0,
                    in1=sdb[:, :], op0=ALU.add, op1=ALU.mult),
                    waits=[(dWp[hc % 2], lg3), (tS, dwn_g.get(gi - 1, 0)), (dR, sdb_rdy),
                           (vS, ln2_done), (vS, vS.n)], inc=vS)
                deq_g[("d", hc)] = dd
                for fi in range(16):
                    for nh in range(2):
                        st("tensor", lambda t, fi=fi, nh=nh, hc=hc: t.matmul(
                            bkf(nh), wgb[:, fi * 128:(fi + 1) * 128],
                            ln2xT[:, fi * 1024 + nh * 512: fi * 1024 + (nh + 1) * 512],
                            start=(fi == 0), stop=(fi == 15)),
                            waits=([(sS, dg), (sS, sil_g.get(hc - 1, 0))]
                                   if fi == 0 and nh == 0 else []),
                            inc=tS if (fi == 15 and nh == 1) else None)
                mm_g[("g", hc)] = tS.n
                for fi in range(16):
                    for nh in range(2):
                        st("tensor", lambda t, fi=fi, nh=nh, hc=hc: t.matmul(
                            bkf(2 + nh), wub[:, fi * 128:(fi + 1) * 128],
                            ln2xT[:, fi * 1024 + nh * 512: fi * 1024 + (nh + 1) * 512],
                            start=(fi == 0), stop=(fi == 15)),
                            waits=([(sS, du), (vS, gum_g.get(hc - 1, 0))]
                                   if fi == 0 and nh == 0 else []),
                            inc=tS if (fi == 15 and nh == 1) else None)
                mm_g[("u", hc)] = tS.n
                sl = st("scalar", lambda s, hc=hc: s.activation(
                    out=gsil[:, :], in_=bkf(0, 2), func=AF.Silu,
                    bias=0.0, scale=sgc[:, hc:hc + 1]),
                    waits=[(tS, mm_g[("g", hc)]), (vS, gum_g.get(hc - 1, 0)),
                           (sS, sS.n)], inc=sS)
                sil_g[hc] = sl
                gm = st("vector", lambda v, hc=hc, jj=jj: v.scalar_tensor_tensor(
                    out=gub[:, jj * 1024:(jj + 1) * 1024], in0=bkf(2, 2),
                    scalar=suc[:, hc:hc + 1], in1=gsil[:, :],
                    op0=ALU.mult, op1=ALU.mult),
                    waits=[(tS, mm_g[("u", hc)]), (sS, sl),
                           (tS, dwn_g.get(gi - 1, 0)), (vS, vS.n)], inc=vS)
                gum_g[hc] = gm
            gum_all = gum_g[grp[-1]]
            deq_all = deq_g[("d", grp[-1])]
            for it in range(16):
                qt, fh = it // 2, it % 2
                pb = 4 if it % 2 == 0 else 6
                guard = dev_g.get((gi, it - 2), dev_g.get((gi - 1, it + 14), ln2_done))
                for jj, hc in enumerate(grp):
                    wdbuf = wdbA if jj < 4 else wdbB
                    wslot = (jj % 4) * D
                    for nh in range(2):
                        st("tensor", lambda t, jj=jj, qt=qt, fh=fh, nh=nh, pb=pb,
                           wdbuf=wdbuf, wslot=wslot, last=(jj == len(grp) - 1): t.matmul(
                            bkf(pb + nh),
                            gub[:, jj * 1024 + qt * 128:jj * 1024 + (qt + 1) * 128],
                            wdbuf[:, wslot + fh * 1024 + nh * 512:
                                  wslot + fh * 1024 + (nh + 1) * 512],
                            start=(jj == 0), stop=last),
                            waits=([(vS, gum_all), (vS, deq_all), (vS, guard)]
                                   if jj == 0 and nh == 0 else []),
                            inc=tS if (jj == len(grp) - 1 and nh == 1) else None)
                mmn = tS.n
                dv = st("vector", lambda v, qt=qt, fh=fh, pb=pb: v.scalar_tensor_tensor(
                    out=res[:, qt * D + fh * 1024: qt * D + (fh + 1) * 1024],
                    in0=bkf(pb, 2), scalar=0.0,
                    in1=res[:, qt * D + fh * 1024: qt * D + (fh + 1) * 1024],
                    op0=ALU.add, op1=ALU.add),
                    waits=[(tS, mmn), (vS, vS.n)], inc=vS)
                dev_g[(gi, it)] = dv
            dwn_g[gi] = tS.n
        ffn_done = (tS.n, sS.n, vS.n)

    if stop_after >= 4:
        fin_w = [(vS, ffn_done[2])]
    elif stop_after == 3:
        fin_w = [(vS, o_done[2])]
    elif stop_after == 2:
        fin_w = [(vS, flash_done[2]), (dRs, res_rdy[0]), (dR, res_rdy[1])]
    else:
        fin_w = [(sS, sS.n), (tS, tS.n), (vS, vS.n)]
    if stop_after >= 2:
        for i in range(8):
            st("sync", lambda e, i=i: e.dma_start(
                out=out[i * 128:(i + 1) * 128, :], in_=res[:, i * D:(i + 1) * D]),
                waits=fin_w if i == 0 else [], inc=dO, k=16)
    else:
        st("sync", lambda e: e.dma_start(out=out[0:128, 0:128], in_=ldst[:, :]),
           waits=fin_w + [(dK, S["dK"].n)], inc=dO, k=16)

    if debug:
        if stop_after >= 1 and stop_after < 4 and sub >= 15:
            st("sync", lambda e: e.dma_start(out=dbg["qT"].ap(), in_=qT[:, :]),
               waits=[(tS, tS.n), (sS, sS.n), (vS, vS.n)], inc=dO, k=16)
        if stop_after >= 1 and sub >= 12:
            st("sync", lambda e: e.dma_start(out=dbg["k"].ap(), in_=kslab.ap()),
               waits=[(dK, S["dK"].n), (tS, tS.n), (sS, sS.n), (vS, vS.n)], inc=dO, k=16)
        if stop_after >= 1 and sub >= 14:
            st("sync", lambda e: e.dma_start(out=dbg["v"].ap(), in_=vslab.ap()), inc=dO, k=16)
        if stop_after >= 1 and sub >= 16:
            st("sync", lambda e: e.dma_start(out=dbg["b"].ap(), in_=bbrow.ap()),
               waits=[(dB, S["dB"].n)], inc=dO, k=16)
        if stop_after >= 2 and stop_after < 4:
            st("sync", lambda e: e.dma_start(out=dbg["f"].ap(), in_=featsT[:, :]), inc=dO, k=16)
        if stop_after >= 3:
            for i in range(8):
                st("sync", lambda e, i=i: e.dma_start(
                    out=dbg["r1"][i * 128:(i + 1) * 128, :], in_=res[:, i * D:(i + 1) * D]),
                   inc=dO, k=16)

    final = dO.n
    st("sync", lambda e: e.wait_ge(dO.h, final))

    with nc.Block() as blk:
        for ename in ENGS:
            def mk(ename):
                def body(e):
                    for (en, fn, waits, incs) in steps:
                        if en != ename:
                            continue
                        for (h, v) in waits:
                            e.wait_ge(h, v)
                        inst = fn(e)
                        for (s_, k_) in incs:
                            inst.then_inc(s_.h, k_)
                return body
            getattr(blk, ename)(mk(ename))
    ctx.close()
    return nc


def prep_inputs(L0_attn_x, L0_attn_Wq, L0_attn_bq, L0_attn_Wk, L0_attn_bk,
                L0_attn_Wv, L0_attn_bv, L0_attn_Wo, L0_attn_bo,
                L0_ffn_wq_gate, L0_ffn_s_gate, L0_ffn_wq_up, L0_ffn_s_up,
                L0_ffn_wq_down, L0_ffn_s_down):
    x = np.asarray(L0_attn_x, dtype=np.float32)
    c = np.ascontiguousarray

    def slab16(w):
        return c(np.asarray(w, np.float32).reshape(D, 16, 128).transpose(1, 0, 2))

    shared = {
        "wq": slab16(L0_attn_Wq), "wk": slab16(L0_attn_Wk), "wv": slab16(L0_attn_Wv),
        "wo": c(np.asarray(L0_attn_Wo, np.float32).reshape(D, 4, 512).transpose(1, 0, 2)),
        "bq": c(np.asarray(L0_attn_bq, np.float32)),
        "bk": c(np.asarray(L0_attn_bk, np.float32)),
        "bv": c(np.asarray(L0_attn_bv, np.float32)),
        "bo": c(np.asarray(L0_attn_bo, np.float32)),
        "wg": c(np.asarray(L0_ffn_wq_gate, np.int32).T.reshape(D, NHC, 128).transpose(1, 0, 2)),
        "wu": c(np.asarray(L0_ffn_wq_up, np.int32).T.reshape(D, NHC, 128).transpose(1, 0, 2)),
        "wd": c(np.asarray(L0_ffn_wq_down, np.int32).T.reshape(NHC, 128, D)),
        "sg": c(np.asarray(L0_ffn_s_gate, np.float32)),
        "su": c(np.asarray(L0_ffn_s_up, np.float32)),
        "sd": c(np.asarray(L0_ffn_s_down, np.float32)),
    }
    in_maps = []
    for core in range(8):
        b, q = core // 4, core % 4
        m = dict(shared)
        m["xb"] = c(np.roll(x[b], -TO * q, axis=0))
        in_maps.append(m)
    return in_maps


def kernel(**inputs):
    in_maps = prep_inputs(**inputs)
    nc = build()
    res = run_bass_kernel_spmd(nc, in_maps, core_ids=list(range(8)))
    outs = [np.asarray(res.results[i]["out"], dtype=np.float32) for i in range(8)]
    return np.concatenate(outs, axis=0).reshape(2, TB, D)


# version 11
# speedup vs baseline: 1.1914x; 1.1914x over previous
import math
import numpy as np
import concourse.bass as bass
import concourse.mybir as mybir
from concourse.bass_utils import run_bass_kernel_spmd

F32 = mybir.dt.float32
BF16 = mybir.dt.bfloat16
I32 = mybir.dt.int32
AF = mybir.ActivationFunctionType
ALU = mybir.AluOpType
AX = mybir.AxisListType

D = 2048
HID = 5632
NH = 32
NP = 16
TB = 4096
TO = 1024
INVS = 1.0 / math.sqrt(32.0)
EPS = 1e-5
NHC = 44
GRP = 8
ENGS = ("sync", "gpsimd", "vector", "scalar", "tensor")


class Sem:
    def __init__(self, h):
        self.h = h
        self.n = 0


def build(stop_after=99, debug=False, sub=99):
    nc = bass.Bass("TRN2", target_bir_lowering=False)

    xb = nc.dram_tensor("xb", [TB, D], F32, kind="ExternalInput")
    wq = nc.dram_tensor("wq", [16, D, 128], F32, kind="ExternalInput")
    wk = nc.dram_tensor("wk", [16, D, 128], F32, kind="ExternalInput")
    wv = nc.dram_tensor("wv", [16, D, 128], F32, kind="ExternalInput")
    wo = nc.dram_tensor("wo", [4, D, 512], F32, kind="ExternalInput")
    bq = nc.dram_tensor("bq", [D], F32, kind="ExternalInput")
    bk = nc.dram_tensor("bk", [D], F32, kind="ExternalInput")
    bv = nc.dram_tensor("bv", [D], F32, kind="ExternalInput")
    bo = nc.dram_tensor("bo", [D], F32, kind="ExternalInput")
    wg = nc.dram_tensor("wg", [NHC, D, 128], I32, kind="ExternalInput")
    wu = nc.dram_tensor("wu", [NHC, D, 128], I32, kind="ExternalInput")
    wd = nc.dram_tensor("wd", [NHC, 128, D], I32, kind="ExternalInput")
    sg = nc.dram_tensor("sg", [HID], F32, kind="ExternalInput")
    su = nc.dram_tensor("su", [HID], F32, kind="ExternalInput")
    sd = nc.dram_tensor("sd", [D], F32, kind="ExternalInput")
    out = nc.dram_tensor("out", [TO, D], F32, kind="ExternalOutput")

    kslab = nc.dram_tensor("kslab", [16, 128, TB], BF16)
    vslab = nc.dram_tensor("vslab", [NH, TB, 64], BF16)
    bbrow = nc.dram_tensor("bbrow", [NH], F32)
    dslab = nc.dram_tensor("dslab", [NH, TO], F32)
    dbg = {}
    if debug:
        dbg["qT"] = nc.dram_tensor("dbg_qT", [128, 16 * TO], BF16, kind="ExternalOutput")
        dbg["k"] = nc.dram_tensor("dbg_k", [16, 128, TB], BF16, kind="ExternalOutput")
        dbg["v"] = nc.dram_tensor("dbg_v", [NH, TB, 64], BF16, kind="ExternalOutput")
        dbg["b"] = nc.dram_tensor("dbg_b", [NH], F32, kind="ExternalOutput")
        dbg["f"] = nc.dram_tensor("dbg_f", [128, 16 * TO], BF16, kind="ExternalOutput")
        dbg["r1"] = nc.dram_tensor("dbg_r1", [TO, D], F32, kind="ExternalOutput")

    PB = 17920
    off = [PB]

    def alloc(name, shape, dtype, at=None):
        a = off[0] if at is None else at
        t = nc.alloc_sbuf_tensor_at(name, shape, dtype, offset=a)
        if at is None:
            per = mybir.dt.size(dtype)
            for s in shape[1:]:
                per *= s
            off[0] = a + ((per + 31) // 32) * 32
        return t

    identb = alloc("identb", [128, 128], BF16)
    identf = alloc("identf", [128, 128], F32)
    biasB = alloc("biasB", [128, NH], F32)
    bqs = alloc("bqs", [128, 16], F32)
    bks = alloc("bks", [128, 16], F32)
    bvs = alloc("bvs", [128, 16], F32)
    kmax = alloc("kmax", [128, NH], F32)
    qmax = alloc("qmax", [128, NH], F32)
    scr = alloc("scr", [128, 64], F32)
    nrow = alloc("nrow", [1, 64], F32)
    stat = alloc("stat", [128, 24], F32)
    mv = alloc("mv", [128, 2], F32)
    rstd = alloc("rstd", [128, 1], F32)
    negmr = alloc("negmr", [128, 1], F32)
    lnv = alloc("lnv", [128, 1], F32)
    sgc = alloc("sgc", [128, NHC], F32)
    suc = alloc("suc", [128, NHC], F32)
    epst = alloc("epst", [128, 1], F32)
    n128 = alloc("n128", [128, 1], F32)
    ldst = alloc("ldst", [128, 128], F32)
    lds2 = alloc("lds2", [128, 128], F32)
    P_END = off[0]
    assert P_END - PB <= 4096, P_END - PB
    R1 = ((P_END + 63) // 64) * 64
    R2 = R1 + 32768
    R3 = R2 + 32768
    SB_LIMIT = 229376
    R3SZ = SB_LIMIT - R3
    assert R3SZ >= 139264, R3SZ

    qT = alloc("qT", [128, 16 * TO], BF16, at=R1)
    featsT = alloc("featsT", [128, 16 * TO], BF16, at=R2)
    lnxT = alloc("lnxT", [128, 16 * 2048], BF16, at=R3)
    lnst = alloc("lnst", [128, 4 * D], BF16, at=R3 + 65536)
    xst = [alloc("xst0", [128, D], F32, at=R3 + 81920),
           alloc("xst1", [128, D], F32, at=R3 + 90112)]
    wst = [alloc("wst0", [128, D], F32, at=R3 + 98304),
           alloc("wst1", [128, D], F32, at=R3 + 106496)]
    wbf = [alloc("wbf0", [128, D], BF16, at=R3 + 114688),
           alloc("wbf1", [128, D], BF16, at=R3 + 118784)]
    kev = alloc("kev", [128, 2048], BF16, at=R3 + 122880)
    sqst = alloc("sqst", [128, 2048], BF16, at=R3 + 126976)
    vts = alloc("vts", [128, 2048], BF16, at=R3 + 131072)
    res = alloc("res", [128, 8 * D], F32, at=R3)
    FL = R3 + 65536
    kp = [alloc("kp0", [128, 2 * TB], BF16, at=FL),
          alloc("kp1", [128, 2 * TB], BF16, at=FL + 16384)]
    va = [alloc("va0", [128, 32 * 2 * 128], BF16, at=FL + 32768),
          alloc("va1", [128, 32 * 2 * 128], BF16, at=FL + 49152)]
    pt = [alloc("pt0", [128, TO], BF16, at=FL + 65536),
          alloc("pt1", [128, TO], BF16, at=FL + 67584)]
    nbc = [alloc("nbc0", [128, TO], F32, at=FL + 69632)]
    dsb = nbc[0][0:1, :]
    rsb = dsb
    owst = alloc("owst", [128, 16 * 512], F32, at=FL)
    owbf = [alloc("owbf0", [128, 16 * 512], BF16, at=FL + 32768),
            alloc("owbf1", [128, 16 * 512], BF16, at=FL + 49152)]
    bob = alloc("bob", [128, D], F32, at=FL + 65536)
    ln2xT = alloc("ln2xT", [128, 16 * TO], BF16, at=R1)
    gub = alloc("gub", [128, GRP * TO], BF16, at=R2)
    ln2st = alloc("ln2st", [128, 4 * D], BF16, at=R2 + 16384)
    wdbA = alloc("wdbA", [128, 4 * D], BF16, at=R2 + 16384)
    wgi = [alloc("wgi0", [128, D], I32, at=FL),
           alloc("wgi1", [128, D], I32, at=FL + 8192)]
    wui = [alloc("wui0", [128, D], I32, at=FL + 16384),
           alloc("wui1", [128, D], I32, at=FL + 24576)]
    wdi = alloc("wdi", [128, D], I32, at=FL + 32768)
    wgb = alloc("wgb", [128, D], BF16, at=FL + 40960)
    wub = alloc("wub", [128, D], BF16, at=FL + 45056)
    wdbB = alloc("wdbB", [128, 4 * D], BF16, at=FL + 49152)
    sdb = alloc("sdb", [128, D], F32, at=FL + 65536)
    gsil = alloc("gsil", [128, TO], BF16, at=FL + 73728)
    assert FL + 73728 <= SB_LIMIT - 32, (FL, SB_LIMIT)

    PSA = nc.alloc_psum_tensor("PSA", [128, 4096], F32)

    def bkf(b, n=1):
        return PSA[:, b * 512:(b + n) * 512]

    def bkb(b, n=1):
        return PSA[:, b * 512:(b + n) * 512].bitcast(BF16)

    from contextlib import ExitStack
    ctx = ExitStack()
    S = {n: Sem(ctx.enter_context(nc.semaphore(n)))
         for n in ["dW0", "dW1", "dX0", "dX1", "dK", "dF0", "dF1", "dB",
                   "dR", "dRs", "dO", "tS", "sS", "vS", "gS"]}
    dK, dB, dR, dRs, dO, tS, sS, vS, gS = (S[k] for k in
        ["dK", "dB", "dR", "dRs", "dO", "tS", "sS", "vS", "gS"])
    dWp = [S["dW0"], S["dW1"]]
    dXp = [S["dX0"], S["dX1"]]
    dFp = [S["dF0"], S["dF1"]]

    steps = []

    def st(eng, fn, waits=(), inc=None, k=1):
        w = [(s.h, v) for (s, v) in waits if v > 0]
        incs = []
        r = None
        if inc is not None:
            incs = [(inc, k)]
            inc.n += k
            r = inc.n
        steps.append((eng, fn, w, incs))
        return r

    vav = [t.ap().rearrange("p (c h f) -> p c h f", h=2, f=128) for t in va]

    for t_, val in ((identb[:, :], 0.0), (identf[:, :], 0.0),
                    (kmax[:, :], 0.0), (qmax[:, :], 0.0),
                    (epst[:, :], EPS), (n128[:, :], -128.0),
                    (ldst[:, :], 0.0), (lds2[:, :], 0.0)):
        st("gpsimd", lambda g, tt=t_, val=val: g.memset(tt, val), inc=gS)
    mset = gS.n
    st("gpsimd", lambda g: g.affine_select(
        out=identb[:, :], in_=identb[:, :], compare_op=ALU.not_equal,
        fill=1.0, base=0, pattern=[[-1, 128]], channel_multiplier=1),
        waits=[(gS, mset)], inc=gS)
    st("gpsimd", lambda g: g.affine_select(
        out=identf[:, :], in_=identf[:, :], compare_op=ALU.not_equal,
        fill=1.0, base=0, pattern=[[-1, 128]], channel_multiplier=1),
        waits=[(gS, mset)], inc=gS)
    ident_done = gS.n
    if sub < 2:
        setup_done = sS.n
    if sub >= 2:
        st("gpsimd", lambda g: g.dma_start(
            out=ldst[0:16, 0:128], in_=bq.ap().rearrange("(a c) -> a c", c=128)),
            waits=[(gS, ident_done)], inc=dR, k=16)
    if sub >= 2:
        st("gpsimd", lambda g: g.dma_start(
            out=ldst[32:48, 0:128], in_=bk.ap().rearrange("(a c) -> a c", c=128)), inc=dR, k=16)
        st("gpsimd", lambda g: g.dma_start(
            out=ldst[64:80, 0:128], in_=bv.ap().rearrange("(a c) -> a c", c=128)), inc=dR, k=16)
        st("gpsimd", lambda g: g.dma_start(
            out=lds2[0:44, 0:128], in_=sg.ap().rearrange("(a c) -> a c", c=128)), inc=dR, k=16)
        st("gpsimd", lambda g: g.dma_start(
            out=lds2[64:108, 0:128], in_=su.ap().rearrange("(a c) -> a c", c=128)), inc=dR, k=16)
        rows_ld = dR.n
        setup_done = sS.n
    if sub >= 3:
        st("tensor", lambda t: t.transpose(bkf(0)[:, 0:128], ldst[:, :], identf[:, :]),
           waits=[(gS, ident_done), (dR, rows_ld)], inc=tS)
        st("tensor", lambda t: t.transpose(bkf(2)[:, 0:128], lds2[:, :], identf[:, :]), inc=tS)
        tr_done = tS.n
    if sub >= 4:
        st("scalar", lambda s: s.activation(out=bqs[:, :], in_=bkf(0)[:, 0:16],
                                            func=AF.Copy, bias=0.0, scale=INVS),
           waits=[(tS, tr_done)], inc=sS)
        st("scalar", lambda s: s.copy(out=bks[:, :], in_=bkf(0)[:, 32:48]), inc=sS)
        st("scalar", lambda s: s.copy(out=bvs[:, :], in_=bkf(0)[:, 64:80]), inc=sS)
        st("scalar", lambda s: s.copy(out=sgc[:, :], in_=bkf(2)[:, 0:NHC]), inc=sS)
        st("scalar", lambda s: s.copy(out=suc[:, :], in_=bkf(2)[:, 64:64 + NHC]), inc=sS)
        setup_done = sS.n

    nrm_free = [0]
    nrm_tr = [0]

    def _norms(sqtile, ntiles, maxdst, fo, sqg):
        nb = ntiles // 4
        for b4 in range(nb):
            bank = 4 + b4
            boff = 0
            for j in range(4):
                st("tensor", lambda t, b4=b4, j=j, bank=bank, boff=boff, sqtile=sqtile: t.transpose(
                    bkb(bank)[:, boff + j * 128:boff + (j + 1) * 128],
                    sqtile[:, (b4 * 4 + j) * 128:(b4 * 4 + j + 1) * 128],
                    identb[:, :]),
                    waits=[(vS, sqg), (vS, nrm_free[0])] if (j == 0 and b4 == 0) else [],
                    inc=tS if j == 3 else None)
            trc = tS.n
            st("vector", lambda v, bank=bank, boff=boff: v.reduce_sum(
                out=scr.ap().rearrange("p (a b) -> p a b", b=2)[:, 0:4, :],
                in_=bkb(bank)[:, boff:boff + 512].rearrange("p (i h f) -> p i h f", h=2, f=64),
                axis=AX.X), waits=[(tS, trc), (vS, vS.n)], inc=vS)
            st("vector", lambda v: v.reduce_max(
                out=scr[:, 8:10],
                in_=scr.ap().rearrange("p (a b) -> p b a", b=2)[:, :, 0:4],
                axis=AX.X), waits=[(vS, vS.n)], inc=vS)
            st("vector", lambda v, fo=fo, maxdst=maxdst: v.tensor_max(
                maxdst[:, 2 * fo:2 * fo + 2], maxdst[:, 2 * fo:2 * fo + 2],
                scr[:, 8:10]), waits=[(vS, vS.n)], inc=vS)
        nrm_free[0] = vS.n

    Wmap = {"q": (wq, bqs, INVS), "k": (wk, bks, 1.0), "v": (wv, bvs, 1.0)}
    def emit_finalize(kvtail):
        st("tensor", lambda t: t.transpose(PSA[0:32, 6 * 512:6 * 512 + 128],
                                           kmax[:, 0:32], identf[:, :]),
           waits=kvtail, inc=tS)
        st("tensor", lambda t: t.transpose(PSA[0:32, 6 * 512 + 128:6 * 512 + 256],
                                           qmax[:, 0:32], identf[:, :]), inc=tS)
        trn = tS.n
        st("vector", lambda v: v.reduce_max(out=scr[0:32, 0:1],
                                            in_=PSA[0:32, 6 * 512:6 * 512 + 128], axis=AX.X),
           waits=[(tS, trn), (vS, vS.n)], inc=vS)
        st("vector", lambda v: v.reduce_max(out=scr[0:32, 1:2],
                                            in_=PSA[0:32, 6 * 512 + 128:6 * 512 + 256], axis=AX.X),
           inc=vS)
        st("vector", lambda v: v.tensor_mul(scr[0:32, 2:3], scr[0:32, 0:1], scr[0:32, 1:2]),
           waits=[(vS, vS.n)], inc=vS)
        mx = vS.n
        st("scalar", lambda s: s.activation(out=scr[0:32, 3:4], in_=scr[0:32, 2:3],
                                            func=AF.Ln, bias=0.0, scale=1.0),
           waits=[(vS, mx), (sS, sS.n)], inc=sS)
        st("scalar", lambda s: s.activation(out=scr[0:32, 4:5], in_=scr[0:32, 3:4],
                                            func=AF.Exp, bias=0.0, scale=0.5),
           waits=[(sS, sS.n)], inc=sS)
        sq_ = sS.n
        st("vector", lambda v: v.tensor_scalar(
            out=scr[0:32, 0:1], in0=scr[0:32, 4:5], scalar1=1.0, scalar2=-1.0,
            op0=ALU.add, op1=ALU.mult), waits=[(sS, sq_), (vS, vS.n)], inc=vS)
        ng = vS.n
        st("tensor", lambda t: t.transpose(PSA[0:64, 7 * 512:7 * 512 + 128],
                                           scr[:, :], identf[:, :]),
           waits=[(vS, ng)], inc=tS)
        st("scalar", lambda s: s.copy(out=nrow[0:1, 0:32],
                                      in_=PSA[0:1, 7 * 512:7 * 512 + 32]),
           waits=[(tS, tS.n), (sS, sS.n)], inc=sS)
        nr = sS.n
        st("gpsimd", lambda g: g.dma_start(out=bbrow.ap().unsqueeze(0), in_=nrow[0:1, 0:32]),
           waits=[(sS, nr)], inc=dB, k=16)
        st("gpsimd", lambda g: g.dma_start(
            out=biasB[:, :], in_=bbrow.ap().unsqueeze(0).partition_broadcast(128)),
           waits=[(dB, dB.n)], inc=dB, k=16)
        return dB.n

    lnx_guard = {}
    kv_end = {}
    kv_done = (0, 0, 0, 0)
    bias_rdy = 0
    if stop_after >= 1:
        for h in (range(2) if sub >= 11 else []):
            prev_cp = vS.n
            for tt in range(16):
                b4, j4 = tt // 4, tt % 4
                g_x = lnx_guard.get((h, tt - 2), lnx_guard.get((h - 1, 14 + tt), 0))
                ldx = st("sync", lambda e, h=h, tt=tt: e.dma_start(
                    out=xst[tt % 2][:, :],
                    in_=xb[h * 2048 + tt * 128: h * 2048 + (tt + 1) * 128, :]),
                    waits=[(sS, g_x)], inc=dXp[tt % 2], k=16)
                for i in range(4):
                    st("vector", lambda v, tt=tt, i=i: v.bn_stats(
                        out=stat[:, 6 * i:6 * i + 6], in_=xst[tt % 2][:, 512 * i:512 * (i + 1)]),
                        waits=[(dXp[tt % 2], ldx), (vS, vS.n)] if i == 0 else [], inc=vS)
                st("vector", lambda v: v.bn_aggr(out=mv[:, :], in_=stat[:, :]),
                   waits=[(vS, vS.n)], inc=vS)
                ag = vS.n
                st("scalar", lambda s: s.activation(out=lnv[:, :], in_=mv[:, 1:2],
                                                    func=AF.Ln, bias=epst[:, :], scale=1.0),
                   waits=[(vS, ag), (sS, sS.n)], inc=sS)
                st("scalar", lambda s: s.activation(out=rstd[:, :], in_=lnv[:, :],
                                                    func=AF.Exp, bias=0.0, scale=-0.5),
                   waits=[(sS, sS.n)], inc=sS)
                rs = sS.n
                st("vector", lambda v: v.scalar_tensor_tensor(
                    out=negmr[:, :], in0=mv[:, 0:1], scalar=-1.0, in1=rstd[:, :],
                    op0=ALU.mult, op1=ALU.mult), waits=[(sS, rs), (vS, vS.n)], inc=vS)
                nm = vS.n
                lnslot = lnst[:, j4 * D:(j4 + 1) * D]
                lg = st("scalar", lambda s, lnslot=lnslot, tt=tt: s.activation(
                    out=lnslot, in_=xst[tt % 2][:, :], func=AF.Identity,
                    bias=negmr[:, :], scale=rstd[:, :]),
                    waits=[(vS, nm), (vS, prev_cp), (sS, sS.n)], inc=sS)
                lnx_guard[(h, tt)] = lg
                if j4 == 3:
                    lnrdy = lg
                    cp_g = {}
                    for fi in range(16):
                        bank = 4 + (fi % 2)
                        for j in range(4):
                            st("tensor", lambda t, fi=fi, j=j, bank=bank: t.transpose(
                                bkb(bank)[:, j * 128:(j + 1) * 128],
                                lnst[:, j * D + fi * 128: j * D + (fi + 1) * 128],
                                identb[:, :]),
                                waits=([(sS, lnrdy), (vS, cp_g.get(fi - 2, prev_cp))]
                                       if j == 0 else []),
                                inc=tS if j == 3 else None)
                        trc = tS.n
                        cp_g[fi] = st("vector", lambda v, fi=fi, bank=bank, h=h, b4=b4: v.tensor_copy(
                            out=lnxT[:, fi * 2048 + b4 * 512: fi * 2048 + (b4 + 1) * 512],
                            in_=bkb(bank)[:, 0:512]),
                            waits=[(tS, trc), (vS, vS.n)], inc=vS)
                    prev_cp = vS.n
                    if b4 == 1:
                        ln_half = vS.n
            ln_done = vS.n

            if sub <= 11:
                ws = []
            elif sub <= 13:
                ws = ["k"]
            elif sub == 14:
                ws = ["k", "v"]
            else:
                ws = (["k", "v", "q"] if h == 0 else ["k", "v"])
            cast_g = {}
            evac_g = {}
            mm_end = {}
            for iw, w in enumerate(ws):
                wt, bcol, wscale = Wmap[w]
                ntq = 1 if w == "q" else 2

                def ldcast(iw, fo, wt):
                    pg = cast_g.get((iw, fo - 2), cast_g.get((iw - 1, 14 + fo), 0))
                    ldw = st("sync", lambda e, wt=wt, fo=fo: e.dma_start(
                        out=wst[fo % 2].ap().rearrange("p (c x) -> p c x", x=128),
                        in_=wt[fo].rearrange("(c p) x -> p c x", p=128)),
                        waits=[(sS, pg)], inc=dWp[fo % 2], k=16)
                    mmg = mm_end.get((iw, fo - 2), mm_end.get((iw - 1, 14 + fo), 0))
                    cg = st("scalar", lambda s, fo=fo: s.copy(
                        out=wbf[fo % 2][:, :], in_=wst[fo % 2][:, :]),
                        waits=[(dWp[fo % 2], ldw), (tS, mmg), (sS, sS.n)], inc=sS)
                    cast_g[(iw, fo)] = cg

                if (iw, 0) not in cast_g:
                    ldcast(iw, 0, wt)
                for fo in range(16):
                    if fo + 1 < 16:
                        ldcast(iw, fo + 1, wt)
                    elif iw + 1 < len(ws):
                        ldcast(iw + 1, 0, Wmap[ws[iw + 1]][0])
                    cg = cast_g[(iw, fo)]
                    for tq in range(ntq):
                        pb = 0 if tq == 0 else 2
                        ev_p = evac_g.get((iw, fo - 1, tq),
                                          evac_g.get((iw - 1, 15, tq), 0))
                        for fi in range(16):
                            for nh in range(2):
                                mvw = lnxT[:, fi * 2048 + tq * 1024 + nh * 512:
                                           fi * 2048 + tq * 1024 + (nh + 1) * 512]
                                ln_g = ln_half if tq == 0 else ln_done
                                st("tensor", lambda t, pb=pb, nh=nh, fo=fo, fi=fi, mvw=mvw: t.matmul(
                                    bkf(pb + nh), wbf[fo % 2][:, fi * 128:(fi + 1) * 128],
                                    mvw, start=(fi == 0), stop=(fi == 15)),
                                    waits=([(sS, cg), (vS, ln_g), (sS, ev_p)]
                                           if fi == 0 and nh == 0 else []),
                                    inc=tS if (fi == 15 and nh == 1) else None)
                        mmc = tS.n
                        mm_end[(iw, fo)] = mmc
                        if w == "q":
                            dst = qT[:, fo * 1024:(fo + 1) * 1024]
                        else:
                            dst = kev[:, tq * 1024:(tq + 1) * 1024]
                        kevg = kv_end.get((h, iw, fo - 1),
                                           kv_end.get((h, iw - 1, 15),
                                                      kv_end.get((h - 1, 2, 15), 0)))
                        ev = st("scalar", lambda s, dst=dst, pb=pb, bcol=bcol, fo=fo,
                                wscale=wscale: s.activation(
                                    out=dst, in_=bkf(pb, 2), func=AF.Identity,
                                    bias=bcol[:, fo:fo + 1], scale=wscale),
                                waits=[(tS, mmc), (dK, kevg), (sS, sS.n)], inc=sS)
                        evac_g[(iw, fo, tq)] = ev
                    evd = sS.n
                    if w == "k":
                        st("gpsimd", lambda g, fo=fo, h=h: g.dma_start(
                            out=kslab[fo][:, h * 2048:(h + 1) * 2048], in_=kev[:, :]),
                            waits=[(sS, evd)], inc=dK, k=16)
                        sqg = st("vector", lambda v: v.tensor_mul(
                            sqst[:, :], kev[:, :], kev[:, :]),
                            waits=[(sS, evd), (vS, vS.n), (tS, nrm_tr[0])], inc=vS)
                        if sub >= 13:
                            _norms(sqst, 16, kmax, fo, sqg)
                        nrm_tr[0] = tS.n
                        kv_end[(h, iw, fo)] = dK.n
                        if h == 1 and fo == 15 and sub >= 16:
                            bias_rdy = emit_finalize([(vS, vS.n), (sS, sS.n)])
                    elif w == "v":
                        for b8 in range(2):
                            bank = 4 + b8
                            for i in range(8):
                                st("tensor", lambda t, b8=b8, i=i, bank=bank: t.transpose(
                                    bkb(bank)[:, i * 128:(i + 1) * 128],
                                    kev[:, (b8 * 8 + i) * 128:(b8 * 8 + i + 1) * 128],
                                    identb[:, :]),
                                    waits=[(sS, evd), (sS, sS.n), (vS, vS.n)] if i == 0 else [],
                                    inc=tS if i == 7 else None)
                            trc = tS.n
                            st("vector", lambda v, b8=b8, bank=bank: v.tensor_copy(
                                out=vts[:, b8 * 1024:(b8 + 1) * 1024], in_=bkb(bank)),
                                waits=[(tS, trc),
                                       (dK, kv_end.get((h, iw, fo - 1),
                                            kv_end.get((h, iw - 1, 15),
                                            kv_end.get((h - 1, 2, 15), 0)))),
                                       (vS, vS.n)], inc=vS)
                        vtd = vS.n
                        for h2 in range(2):
                            for tt8 in range(2):
                                src = vts.ap().rearrange("p (c x) -> p c x", x=128)[
                                    :, tt8 * 8:(tt8 + 1) * 8, h2 * 64:(h2 + 1) * 64]
                                dsl = vslab[2 * fo + h2].rearrange("(c p) f -> p c f", p=128)[
                                    :, h * 16 + tt8 * 8: h * 16 + (tt8 + 1) * 8, :]
                                st("gpsimd", lambda g, src=src, dsl=dsl: g.dma_start(
                                    out=dsl, in_=src),
                                    waits=[(vS, vtd)] if h2 == 0 and tt8 == 0 else [],
                                    inc=dK, k=16)
                        kv_end[(h, iw, fo)] = dK.n
                    else:
                        sqg = st("vector", lambda v, fo=fo: v.tensor_mul(
                            sqst[:, 0:1024], qT[:, fo * 1024:(fo + 1) * 1024],
                            qT[:, fo * 1024:(fo + 1) * 1024]),
                            waits=[(sS, evd), (vS, vS.n), (tS, nrm_tr[0])], inc=vS)
                        if sub >= 13:
                            _norms(sqst, 8, qmax, fo, sqg)
                        nrm_tr[0] = tS.n
                        kv_end[(h, iw, fo)] = dK.n


    if stop_after >= 1:
        kv_done = (tS.n, sS.n, vS.n, dK.n)

    flash_done = (0, 0, 0)
    res_rdy = 0
    if stop_after >= 2:
        kvw = [(tS, kv_done[0]), (sS, kv_done[1]), (vS, kv_done[2])]
        st("gpsimd", lambda g: g.memset(kp[0][:, :], 0.0),
           waits=kvw + [(dK, kv_done[3])], inc=gS)
        st("gpsimd", lambda g: g.memset(kp[1][:, :], 0.0), inc=gS)
        st("gpsimd", lambda g: g.memset(va[0][:, :], 0.0), inc=gS)
        st("gpsimd", lambda g: g.memset(va[1][:, :], 0.0), inc=gS)
        zro = gS.n
        st("gpsimd", lambda g: g.memset(vav[0][:, :, :, 64:65], 1.0),
           waits=[(gS, zro)], inc=gS)
        st("gpsimd", lambda g: g.memset(vav[1][:, :, :, 64:65], 1.0), inc=gS)
        va_ones = gS.n
        for i in range(8):
            st("sync", lambda e, i=i: e.dma_start(
                out=res[:, i * D:(i + 1) * D], in_=xb[i * 128:(i + 1) * 128, :]),
                waits=kvw if i == 0 else [], inc=dRs, k=16)
        res_s = dRs.n
        res_rdy = (res_s, 0)

        pair_end = {}

        def prefetch(p):
            wts = [(tS, pair_end.get(p - 2, -1))]
            if p <= 1:
                wts += [(dK, kv_done[3]), (gS, va_ones)]
            st("sync", lambda e, p=p: e.dma_start(
                out=kp[p % 2][0:64, 0:TB], in_=kslab[p][0:64, :]),
               waits=wts, inc=dFp[p % 2], k=16)
            st("sync", lambda e, p=p: e.dma_start(
                out=kp[p % 2][64:128, TB:2 * TB], in_=kslab[p][64:128, :]),
               inc=dFp[p % 2], k=16)
            for h2 in range(2):
                for hf in range(4):
                    src = vslab[2 * p + h2].rearrange("(c p) f -> p c f", p=128)[
                        :, hf * 8:(hf + 1) * 8, :]
                    dstv = vav[p % 2][:, hf * 8:(hf + 1) * 8, h2, 0:64]
                    st("sync", lambda e, src=src, dstv=dstv: e.dma_start(
                        out=dstv, in_=src), inc=dFp[p % 2], k=16)

        prefetch(0)
        prefetch(1)
        fn_guard = {}
        nrm_g = {}
        rc_g = {}
        rsb_dma = {}
        prev_exp30 = 0
        prev_eb = 0
        prev_av = None
        for p in range(NP):
            need_dF = 160 * (p // 2 + 1)
            for h2 in range(2):
                d = 2 * p + h2
                acb = 4 if h2 == 0 else 6
                exp_cnt = {}
                av_cnt = {}
                qk_cnt = {}

                def emit_qk(c, wts):
                    for qh in range(2):
                        st("tensor", lambda t, p=p, h2=h2, c=c, qh=qh: t.matmul(
                            bkf((0 if c % 2 == 0 else 2) + qh),
                            kp[p % 2][:, h2 * TB + c * 128: h2 * TB + (c + 1) * 128],
                            qT[:, p * 1024 + qh * 512: p * 1024 + (qh + 1) * 512],
                            start=True, stop=True),
                            waits=wts if qh == 0 else [], inc=tS if qh == 1 else None)
                    qk_cnt[c] = tS.n

                emit_qk(0, [(dFp[p % 2], need_dF), (sS, prev_exp30),
                            (dB, bias_rdy), (gS, va_ones)])
                emit_qk(1, [(sS, prev_eb)])
                for c in range(32):
                    avw = [(tS, qk_cnt[c])]
                    if c >= 2:
                        avw.append((tS, av_cnt[c - 2]))
                    elif prev_av is not None:
                        avw.append((tS, prev_av[30 + c]))
                    e = st("scalar", lambda s, p=p, c=c, d=d: s.activation(
                        out=pt[c % 2][:, :], in_=bkf(0 if c % 2 == 0 else 2, 2),
                        func=AF.Exp, bias=biasB[:, d:d + 1], scale=1.0),
                        waits=avw, inc=sS)
                    exp_cnt[c] = e
                    if c + 2 <= 31:
                        emit_qk(c + 2, [(sS, e)])
                    avw2 = [(sS, e)]
                    if c == 0:
                        avw2.append((vS, fn_guard.get((p, h2), 0)))
                    for qh in range(2):
                        st("tensor", lambda t, p=p, h2=h2, c=c, qh=qh, acb=acb: t.matmul(
                            PSA[:, (acb + qh) * 512:(acb + qh + 1) * 512],
                            vav[p % 2][:, c, h2, 0:128],
                            pt[c % 2][:, qh * 512:(qh + 1) * 512],
                            start=(c == 0), stop=(c == 31)),
                            waits=avw2 if qh == 0 else [],
                            inc=tS if qh == 1 else None)
                    av_cnt[c] = tS.n
                cd = st("scalar", lambda s, acb=acb: s.copy(
                    out=dsb, in_=PSA[64:65, acb * 512:(acb + 2) * 512]),
                    waits=[(tS, av_cnt[31]), (sS, sS.n), (vS, nrm_g.get(d - 1, 0)),
                           (dR, rsb_dma.get(d - 1, 0))],
                    inc=sS)
                fg = st("vector", lambda v, p=p, h2=h2, acb=acb: v.tensor_copy(
                    out=featsT[h2 * 64:(h2 + 1) * 64, p * 1024:(p + 1) * 1024],
                    in0=PSA[0:64, acb * 512:(acb + 2) * 512]) if False else v.tensor_copy(
                    out=featsT[h2 * 64:(h2 + 1) * 64, p * 1024:(p + 1) * 1024],
                    in_=PSA[0:64, acb * 512:(acb + 2) * 512]),
                    waits=[(tS, av_cnt[31]), (vS, vS.n)], inc=vS)
                fn_guard[(p + 1, h2)] = fg
                rc = st("vector", lambda v: v.reciprocal(out=rsb, in_=dsb),
                        waits=[(sS, cd), (vS, vS.n), (dR, rsb_dma.get(d - 1, 0))], inc=vS)
                rc_g[d] = rc
                st("gpsimd", lambda g, d=d: g.dma_start(
                    out=dslab[d].unsqueeze(0), in_=rsb),
                    waits=[(vS, rc)], inc=dR, k=16)
                rsb_dma[d] = dR.n
                st("gpsimd", lambda g, d=d, h2=h2: g.dma_start(
                    out=nbc[0][h2 * 64:(h2 + 1) * 64, :],
                    in_=dslab[d].unsqueeze(0).partition_broadcast(64)),
                    waits=[(dR, dR.n), (vS, nrm_g.get(d - 1, 0))], inc=dR, k=16)
                bcd = dR.n
                ng_ = st("vector", lambda v, p=p, h2=h2, d=d: v.tensor_tensor(
                    out=featsT[h2 * 64:(h2 + 1) * 64, p * 1024:(p + 1) * 1024],
                    in0=featsT[h2 * 64:(h2 + 1) * 64, p * 1024:(p + 1) * 1024],
                    in1=nbc[0][h2 * 64:(h2 + 1) * 64, :], op=ALU.mult),
                    waits=[(dR, bcd), (vS, fg), (vS, vS.n)], inc=vS)
                nrm_g[d] = ng_
                prev_exp30 = exp_cnt[30]
                prev_eb = exp_cnt[31]
                prev_av = av_cnt
                if h2 == 1:
                    pair_end[p] = tS.n
                    if p + 2 < NP:
                        prefetch(p + 2)
        flash_done = (tS.n, sS.n, vS.n)

    o_done = (0, 0, 0)
    if stop_after >= 3:
        st("gpsimd", lambda g: g.dma_start(
            out=bob[:, :], in_=bo.ap().unsqueeze(0).partition_broadcast(128)),
            waits=[(sS, flash_done[1]), (vS, flash_done[2]), (tS, flash_done[0])],
            inc=dR, k=16)
        bob_ld = dR.n
        st("vector", lambda v: v.tensor_add(res[:, 0:D], res[:, 0:D], bob[:, :]),
           waits=[(dRs, res_rdy[0]), (dR, bob_ld), (vS, flash_done[2])], inc=vS)
        for i in range(1, 8):
            st("vector", lambda v, i=i: v.tensor_add(
                res[:, i * D:(i + 1) * D], res[:, i * D:(i + 1) * D], bob[:, :]), inc=vS)
        bob_done = vS.n
        ocast_g = {}
        omm_g = {}
        oev_g = {}
        for fob in range(4):
            wts = [(sS, ocast_g.get(fob - 1, 0))]
            if fob == 0:
                wts += [(tS, flash_done[0]), (sS, flash_done[1]), (vS, flash_done[2])]
            ldw = st("sync", lambda e, fob=fob: e.dma_start(
                out=owst.ap().rearrange("p (c x) -> p c x", x=512),
                in_=wo[fob].rearrange("(c p) x -> p c x", p=128)),
                waits=wts, inc=dWp[fob % 2], k=16)
            cg = st("scalar", lambda s, fob=fob: s.copy(
                out=owbf[fob % 2][:, :], in_=owst[:, :]),
                waits=[(dWp[fob % 2], ldw), (tS, omm_g.get((fob - 2, 7), flash_done[0])),
                       (sS, flash_done[1]), (sS, sS.n)], inc=sS)
            ocast_g[fob] = cg
            for qt in range(8):
                pb = qt % 2
                guard = oev_g.get((fob, qt - 2), oev_g.get((fob - 1, qt + 6), bob_done))
                for fi in range(16):
                    st("tensor", lambda t, fob=fob, qt=qt, fi=fi, pb=pb: t.matmul(
                        bkf(pb), featsT[:, fi * 1024 + qt * 128: fi * 1024 + (qt + 1) * 128],
                        owbf[fob % 2][:, fi * 512:(fi + 1) * 512],
                        start=(fi == 0), stop=(fi == 15)),
                        waits=([(sS, cg), (vS, guard), (sS, flash_done[1])]
                               if fi == 0 else []),
                        inc=tS if fi == 15 else None)
                omm_g[(fob, qt)] = tS.n
                ev = st("vector", lambda v, fob=fob, qt=qt, pb=pb: v.scalar_tensor_tensor(
                    out=res[:, qt * D + fob * 512: qt * D + (fob + 1) * 512],
                    in0=bkf(pb), scalar=0.0,
                    in1=res[:, qt * D + fob * 512: qt * D + (fob + 1) * 512],
                    op0=ALU.add, op1=ALU.add),
                    waits=[(tS, tS.n), (vS, vS.n)], inc=vS)
                oev_g[(fob, qt)] = ev
        o_done = (tS.n, sS.n, vS.n)

    ffn_done = (0, 0, 0)
    if stop_after >= 4:
        st("gpsimd", lambda g: g.dma_start(
            out=sdb[:, :], in_=sd.ap().unsqueeze(0).partition_broadcast(128)),
            waits=[(vS, o_done[2])], inc=dR, k=16)
        sdb_rdy = dR.n
        prev_cp = o_done[2]
        for tt in range(8):
            b4, j4 = tt // 4, tt % 4
            rv = res[:, tt * D:(tt + 1) * D]
            for i in range(4):
                st("vector", lambda v, rv=rv, i=i: v.bn_stats(
                    out=stat[:, 6 * i:6 * i + 6], in_=rv[:, 512 * i:512 * (i + 1)]),
                    waits=[(vS, vS.n)] if i == 0 else [], inc=vS)
            st("vector", lambda v: v.bn_aggr(out=mv[:, :], in_=stat[:, :]),
               waits=[(vS, vS.n)], inc=vS)
            ag = vS.n
            st("scalar", lambda s: s.activation(out=lnv[:, :], in_=mv[:, 1:2],
                                                func=AF.Ln, bias=epst[:, :], scale=1.0),
               waits=[(vS, ag), (sS, sS.n), (sS, o_done[1])], inc=sS)
            st("scalar", lambda s: s.activation(out=rstd[:, :], in_=lnv[:, :],
                                                func=AF.Exp, bias=0.0, scale=-0.5),
               waits=[(sS, sS.n)], inc=sS)
            rs = sS.n
            st("vector", lambda v: v.scalar_tensor_tensor(
                out=negmr[:, :], in0=mv[:, 0:1], scalar=-1.0, in1=rstd[:, :],
                op0=ALU.mult, op1=ALU.mult), waits=[(sS, rs), (vS, vS.n)], inc=vS)
            nm = vS.n
            lnslot = ln2st[:, j4 * D:(j4 + 1) * D]
            lg = st("scalar", lambda s, lnslot=lnslot, rv=rv: s.activation(
                out=lnslot, in_=rv, func=AF.Identity, bias=negmr[:, :], scale=rstd[:, :]),
                waits=[(vS, nm), (vS, prev_cp), (sS, sS.n)], inc=sS)
            if j4 == 3:
                cp_g = {}
                for fi in range(16):
                    bank = 4 + (fi % 2)
                    for j in range(4):
                        st("tensor", lambda t, fi=fi, j=j, bank=bank: t.transpose(
                            bkb(bank)[:, j * 128:(j + 1) * 128],
                            ln2st[:, j * D + fi * 128: j * D + (fi + 1) * 128],
                            identb[:, :]),
                            waits=([(sS, lg), (vS, cp_g.get(fi - 2, prev_cp)),
                                    (tS, o_done[0])] if j == 0 else []),
                            inc=tS if j == 3 else None)
                    trc = tS.n
                    cp_g[fi] = st("vector", lambda v, fi=fi, bank=bank, b4=b4: v.tensor_copy(
                        out=ln2xT[:, fi * 1024 + b4 * 512: fi * 1024 + (b4 + 1) * 512],
                        in_=bkb(bank)[:, 0:512]),
                        waits=[(tS, trc), (vS, vS.n)], inc=vS)
                prev_cp = vS.n
        ln2_done = vS.n

        deq_g = {}
        mm_g = {}
        sil_g = {}
        gum_g = {}
        dwn_g = {}
        dev_g = {}
        groups = [list(range(g0, min(g0 + GRP, NHC))) for g0 in range(0, NHC, GRP)]
        for gi, grp in enumerate(groups):
            for jj, hc in enumerate(grp):
                ow = ([(sS, o_done[1]), (tS, o_done[0]), (vS, o_done[2])]
                      if hc < 2 else [])
                lg1 = st("sync", lambda e, hc=hc: e.dma_start(
                    out=wgi[hc % 2].ap().rearrange("p (c x) -> p c x", x=128),
                    in_=wg[hc].rearrange("(c p) x -> p c x", p=128)),
                    waits=[(sS, deq_g.get(("g", hc - 2), 0))] + ow,
                    inc=dWp[hc % 2], k=16)
                lg2 = st("sync", lambda e, hc=hc: e.dma_start(
                    out=wui[hc % 2].ap().rearrange("p (c x) -> p c x", x=128),
                    in_=wu[hc].rearrange("(c p) x -> p c x", p=128)),
                    waits=[(sS, deq_g.get(("u", hc - 2), 0))], inc=dWp[hc % 2], k=16)
                lg3 = st("sync", lambda e, hc=hc: e.dma_start(
                    out=wdi[:, :], in_=wd[hc][:, :]),
                    waits=[(vS, deq_g.get(("d", hc - 1), 0))], inc=dWp[hc % 2], k=16)
                dg = st("scalar", lambda s, hc=hc: s.activation(
                    out=wgb[:, :], in_=wgi[hc % 2][:, :], func=AF.Identity,
                    bias=n128[:, :], scale=1.0),
                    waits=[(dWp[hc % 2], lg3), (tS, mm_g.get(("g", hc - 1), 0)),
                           (vS, ln2_done), (sS, sS.n)], inc=sS)
                deq_g[("g", hc)] = dg
                du = st("scalar", lambda s, hc=hc: s.activation(
                    out=wub[:, :], in_=wui[hc % 2][:, :], func=AF.Identity,
                    bias=n128[:, :], scale=1.0),
                    waits=[(dWp[hc % 2], lg3), (tS, mm_g.get(("u", hc - 1), 0)), (sS, sS.n)],
                    inc=sS)
                deq_g[("u", hc)] = du
                wdbuf = wdbA if jj < 4 else wdbB
                wslot = (jj % 4) * D
                dd = st("vector", lambda v, wdbuf=wdbuf, wslot=wslot: v.scalar_tensor_tensor(
                    out=wdbuf[:, wslot:wslot + D], in0=wdi[:, :], scalar=-128.0,
                    in1=sdb[:, :], op0=ALU.add, op1=ALU.mult),
                    waits=[(dWp[hc % 2], lg3), (tS, dwn_g.get(gi - 1, 0)), (dR, sdb_rdy),
                           (vS, ln2_done), (vS, vS.n)], inc=vS)
                deq_g[("d", hc)] = dd
                for fi in range(16):
                    for nh in range(2):
                        st("tensor", lambda t, fi=fi, nh=nh, hc=hc: t.matmul(
                            bkf(nh), wgb[:, fi * 128:(fi + 1) * 128],
                            ln2xT[:, fi * 1024 + nh * 512: fi * 1024 + (nh + 1) * 512],
                            start=(fi == 0), stop=(fi == 15)),
                            waits=([(sS, dg), (sS, sil_g.get(hc - 1, 0))]
                                   if fi == 0 and nh == 0 else []),
                            inc=tS if (fi == 15 and nh == 1) else None)
                mm_g[("g", hc)] = tS.n
                for fi in range(16):
                    for nh in range(2):
                        st("tensor", lambda t, fi=fi, nh=nh, hc=hc: t.matmul(
                            bkf(2 + nh), wub[:, fi * 128:(fi + 1) * 128],
                            ln2xT[:, fi * 1024 + nh * 512: fi * 1024 + (nh + 1) * 512],
                            start=(fi == 0), stop=(fi == 15)),
                            waits=([(sS, du), (vS, gum_g.get(hc - 1, 0))]
                                   if fi == 0 and nh == 0 else []),
                            inc=tS if (fi == 15 and nh == 1) else None)
                mm_g[("u", hc)] = tS.n
                sl = st("scalar", lambda s, hc=hc: s.activation(
                    out=gsil[:, :], in_=bkf(0, 2), func=AF.Silu,
                    bias=0.0, scale=sgc[:, hc:hc + 1]),
                    waits=[(tS, mm_g[("g", hc)]), (vS, gum_g.get(hc - 1, 0)),
                           (sS, sS.n)], inc=sS)
                sil_g[hc] = sl
                gm = st("vector", lambda v, hc=hc, jj=jj: v.scalar_tensor_tensor(
                    out=gub[:, jj * 1024:(jj + 1) * 1024], in0=bkf(2, 2),
                    scalar=suc[:, hc:hc + 1], in1=gsil[:, :],
                    op0=ALU.mult, op1=ALU.mult),
                    waits=[(tS, mm_g[("u", hc)]), (sS, sl),
                           (tS, dwn_g.get(gi - 1, 0)), (vS, vS.n)], inc=vS)
                gum_g[hc] = gm
            gum_all = gum_g[grp[-1]]
            deq_all = deq_g[("d", grp[-1])]
            for it in range(16):
                qt, fh = it // 2, it % 2
                pb = 4 if it % 2 == 0 else 6
                guard = dev_g.get((gi, it - 2), dev_g.get((gi - 1, it + 14), ln2_done))
                for jj, hc in enumerate(grp):
                    wdbuf = wdbA if jj < 4 else wdbB
                    wslot = (jj % 4) * D
                    for nh in range(2):
                        st("tensor", lambda t, jj=jj, qt=qt, fh=fh, nh=nh, pb=pb,
                           wdbuf=wdbuf, wslot=wslot, last=(jj == len(grp) - 1): t.matmul(
                            bkf(pb + nh),
                            gub[:, jj * 1024 + qt * 128:jj * 1024 + (qt + 1) * 128],
                            wdbuf[:, wslot + fh * 1024 + nh * 512:
                                  wslot + fh * 1024 + (nh + 1) * 512],
                            start=(jj == 0), stop=last),
                            waits=([(vS, gum_all), (vS, deq_all), (vS, guard)]
                                   if jj == 0 and nh == 0 else []),
                            inc=tS if (jj == len(grp) - 1 and nh == 1) else None)
                mmn = tS.n
                dv = st("vector", lambda v, qt=qt, fh=fh, pb=pb: v.scalar_tensor_tensor(
                    out=res[:, qt * D + fh * 1024: qt * D + (fh + 1) * 1024],
                    in0=bkf(pb, 2), scalar=0.0,
                    in1=res[:, qt * D + fh * 1024: qt * D + (fh + 1) * 1024],
                    op0=ALU.add, op1=ALU.add),
                    waits=[(tS, mmn), (vS, vS.n)], inc=vS)
                dev_g[(gi, it)] = dv
            dwn_g[gi] = tS.n
        ffn_done = (tS.n, sS.n, vS.n)

    if stop_after >= 4:
        fin_w = [(vS, ffn_done[2])]
    elif stop_after == 3:
        fin_w = [(vS, o_done[2])]
    elif stop_after == 2:
        fin_w = [(vS, flash_done[2]), (dRs, res_rdy[0]), (dR, res_rdy[1])]
    else:
        fin_w = [(sS, sS.n), (tS, tS.n), (vS, vS.n)]
    if stop_after >= 2:
        for i in range(8):
            st("sync", lambda e, i=i: e.dma_start(
                out=out[i * 128:(i + 1) * 128, :], in_=res[:, i * D:(i + 1) * D]),
                waits=fin_w if i == 0 else [], inc=dO, k=16)
    else:
        st("sync", lambda e: e.dma_start(out=out[0:128, 0:128], in_=ldst[:, :]),
           waits=fin_w + [(dK, S["dK"].n)], inc=dO, k=16)

    if debug:
        if stop_after >= 1 and stop_after < 4 and sub >= 15:
            st("sync", lambda e: e.dma_start(out=dbg["qT"].ap(), in_=qT[:, :]),
               waits=[(tS, tS.n), (sS, sS.n), (vS, vS.n)], inc=dO, k=16)
        if stop_after >= 1 and sub >= 12:
            st("sync", lambda e: e.dma_start(out=dbg["k"].ap(), in_=kslab.ap()),
               waits=[(dK, S["dK"].n), (tS, tS.n), (sS, sS.n), (vS, vS.n)], inc=dO, k=16)
        if stop_after >= 1 and sub >= 14:
            st("sync", lambda e: e.dma_start(out=dbg["v"].ap(), in_=vslab.ap()), inc=dO, k=16)
        if stop_after >= 1 and sub >= 16:
            st("sync", lambda e: e.dma_start(out=dbg["b"].ap(), in_=bbrow.ap()),
               waits=[(dB, S["dB"].n)], inc=dO, k=16)
        if stop_after >= 2 and stop_after < 4:
            st("sync", lambda e: e.dma_start(out=dbg["f"].ap(), in_=featsT[:, :]), inc=dO, k=16)
        if stop_after >= 3:
            for i in range(8):
                st("sync", lambda e, i=i: e.dma_start(
                    out=dbg["r1"][i * 128:(i + 1) * 128, :], in_=res[:, i * D:(i + 1) * D]),
                   inc=dO, k=16)

    final = dO.n
    st("sync", lambda e: e.wait_ge(dO.h, final))

    with nc.Block() as blk:
        for ename in ENGS:
            def mk(ename):
                def body(e):
                    for (en, fn, waits, incs) in steps:
                        if en != ename:
                            continue
                        for (h, v) in waits:
                            e.wait_ge(h, v)
                        inst = fn(e)
                        for (s_, k_) in incs:
                            inst.then_inc(s_.h, k_)
                return body
            getattr(blk, ename)(mk(ename))
    ctx.close()
    return nc


def prep_inputs(L0_attn_x, L0_attn_Wq, L0_attn_bq, L0_attn_Wk, L0_attn_bk,
                L0_attn_Wv, L0_attn_bv, L0_attn_Wo, L0_attn_bo,
                L0_ffn_wq_gate, L0_ffn_s_gate, L0_ffn_wq_up, L0_ffn_s_up,
                L0_ffn_wq_down, L0_ffn_s_down):
    x = np.asarray(L0_attn_x, dtype=np.float32)
    c = np.ascontiguousarray

    def slab16(w):
        return c(np.asarray(w, np.float32).reshape(D, 16, 128).transpose(1, 0, 2))

    shared = {
        "wq": slab16(L0_attn_Wq), "wk": slab16(L0_attn_Wk), "wv": slab16(L0_attn_Wv),
        "wo": c(np.asarray(L0_attn_Wo, np.float32).reshape(D, 4, 512).transpose(1, 0, 2)),
        "bq": c(np.asarray(L0_attn_bq, np.float32)),
        "bk": c(np.asarray(L0_attn_bk, np.float32)),
        "bv": c(np.asarray(L0_attn_bv, np.float32)),
        "bo": c(np.asarray(L0_attn_bo, np.float32)),
        "wg": c(np.asarray(L0_ffn_wq_gate, np.int32).T.reshape(D, NHC, 128).transpose(1, 0, 2)),
        "wu": c(np.asarray(L0_ffn_wq_up, np.int32).T.reshape(D, NHC, 128).transpose(1, 0, 2)),
        "wd": c(np.asarray(L0_ffn_wq_down, np.int32).T.reshape(NHC, 128, D)),
        "sg": c(np.asarray(L0_ffn_s_gate, np.float32)),
        "su": c(np.asarray(L0_ffn_s_up, np.float32)),
        "sd": c(np.asarray(L0_ffn_s_down, np.float32)),
    }
    in_maps = []
    for core in range(8):
        b, q = core // 4, core % 4
        m = dict(shared)
        m["xb"] = c(np.roll(x[b], -TO * q, axis=0))
        in_maps.append(m)
    return in_maps


def kernel(**inputs):
    in_maps = prep_inputs(**inputs)
    nc = build()
    res = run_bass_kernel_spmd(nc, in_maps, core_ids=list(range(8)))
    outs = [np.asarray(res.results[i]["out"], dtype=np.float32) for i in range(8)]
    return np.concatenate(outs, axis=0).reshape(2, TB, D)
